# version 11
# speedup vs baseline: 1.5001x; 1.0469x over previous
import math
from contextlib import ExitStack

import numpy as np
import ml_dtypes
import concourse.bass as bass
import concourse.bacc as bacc
import concourse.mybir as mybir
from concourse.bass_utils import run_bass_kernel_spmd
from concourse.bass_core_rust import IndirectOffsetOnAxis

F32 = mybir.dt.float32
BF16 = mybir.dt.bfloat16
I32 = mybir.dt.int32
AF = mybir.ActivationFunctionType
ALU = mybir.AluOpType
AX = mybir.AxisListType
ET = mybir.EngineType
ts = bass.ts

D = 1024
NCH = 8
NEGBIG = -8.0e9
LN32 = math.log(32.0)
BF = ml_dtypes.bfloat16

FCONSTS = [LN32, 1.0e-5, 0.125, -0.5, 0.5, -1.0, 2.0, 1.0 / 1024.0, 1.0e-8,
           1.0e-12, -1.0e30, 1.0e30, 0.0625]

BAR_PE_ACT = [ET.PE, ET.Activation]
BAR_PE_DVE = [ET.PE, ET.DVE]
BAR_PE_DVE_ACT = [ET.PE, ET.DVE, ET.Activation]
BAR_DVE_ACT = [ET.DVE, ET.Activation]
BAR_DVE_POOL = [ET.DVE, ET.Pool]


def host_constants():
    c = {}
    c["ident"] = np.eye(128, dtype=BF)
    c["identf"] = np.eye(128, dtype=np.float32)
    c["ones128_bf"] = np.ones((128, 1), dtype=BF)
    c["ones1_f"] = np.ones((1, 128), dtype=np.float32)
    i0 = np.arange(128)[:, None]
    j = np.arange(512)[None, :]
    for p in (0, 1):
        i = i0 + 128 * p + 256
        allow = (j >= i - 256) & (j <= i)
        m = np.where(allow, 0.0, NEGBIG).astype(np.float32)
        c[f"mask{p}"] = m
        mf = m.copy()
        mf[:, :256] = NEGBIG
        c[f"maskf{p}"] = mf
    tri = (np.arange(128)[:, None] < np.arange(128)[None, :])
    c["ltri_f"] = np.ascontiguousarray(tri.astype(np.float32))
    c["iota8"] = np.broadcast_to(np.arange(8, dtype=np.float32), (128, 8)).copy()
    c["c32_f"] = np.full((1, 128), 32.0, dtype=np.float32)
    fc = np.zeros((1, 64), dtype=np.float32)
    fc[0, :len(FCONSTS)] = FCONSTS
    c["fconsts"] = fc
    return c


class Ctx:
    def __init__(self, nc, stack):
        self.nc = nc
        self.stack = stack
        self.scopes = [stack]
        self.sems = {}
        self.counts = {}
        self.edges = {}

    def push(self):
        s = ExitStack()
        self.scopes.append(s)
        return s

    def pop(self):
        self.scopes.pop().close()

    def sem(self, name):
        if name not in self.sems:
            self.sems[name] = self.stack.enter_context(self.nc.semaphore(name))
            self.counts[name] = 0
        return self.sems[name]

    def inc(self, inst, name, k=1):
        inst.then_inc(self.sem(name), k)
        self.counts[name] += k
        return inst

    def sb(self, name, shp, dt):
        return self.scopes[-1].enter_context(self.nc.sbuf_tensor(name, shp, dt))

    def ps(self, name, shp, dt):
        return self.scopes[-1].enter_context(self.nc.psum_tensor(name, shp, dt))

    def bar(self, engines=None):
        if engines is None:
            self.nc.all_engine_barrier()
        else:
            self.nc.multi_engine_barrier(engines)

    def pipe(self, name, prod, cons, precharge=1):
        if name in self.edges:
            return self.edges[name]
        p = Pipe(self, name, prod, cons, precharge)
        self.edges[name] = p
        return p

    def mk_edge(self, eng, name):
        sem = self.sem(name)
        reg = eng.alloc_register(name + "_r")
        eng.reg_mov(reg, 0)
        def edge(inst):
            inst.then_inc(sem, 1)
            eng.reg_add(reg, reg, 1)
            eng.wait_ge(sem, reg)
            return inst
        return edge

    def setup_consts(self, fconsts_dram):
        row = self.sb("fconst_row", [1, 64], F32)
        dma_wait(self, self.nc.sync, "fconst", [(row[:], fconsts_dram)])
        for k, val in enumerate(FCONSTS):
            self.nc.const_aps.register(F32, float(val), row[0:1, k:k + 1])
        self.nc.all_engine_barrier()


class Pipe:
    def __init__(self, ctx, name, prod, cons, precharge=1):
        self.mm = ctx.sem(name + "_m")
        self.ev = ctx.sem(name + "_e")
        self.rp = prod.alloc_register(name + "_rp")
        self.rc = cons.alloc_register(name + "_rc")
        prod.reg_mov(self.rp, 0)
        cons.reg_mov(self.rc, 0)
        inst = cons.sem_inc(self.ev, precharge)
        ctx.counts[name + "_e"] += precharge
        self.prod, self.cons = prod, cons
        self.msem_name = name + "_m"
        self.esem_name = name + "_e"
        self.ctx = ctx

    def before_fill(self):
        self.prod.reg_add(self.rp, self.rp, 1)
        self.prod.wait_ge(self.ev, self.rp)

    def after_fill(self, inst):
        inst.then_inc(self.mm, 1)
        self.ctx.counts[self.msem_name] += 1

    def before_drain(self):
        self.cons.reg_add(self.rc, self.rc, 1)
        self.cons.wait_ge(self.mm, self.rc)

    def after_drain(self, inst):
        inst.then_inc(self.ev, 1)
        self.ctx.counts[self.esem_name] += 1


def dma_wait(ctx, eng, name, pairs, engine_kind="plain"):
    s = ctx.sem(name)
    for out, in_ in pairs:
        ctx.inc(eng.dma_start(out=out, in_=in_), name, 16)
    eng.wait_ge(s, ctx.counts[name])


def chunk(tile, c, w=512):
    return tile[:, c * w:(c + 1) * w]


def bt_tiled(x_core_T):
    xT = np.ascontiguousarray(np.asarray(x_core_T, np.float32).T)
    T = xT.shape[1]
    NT = T // 512
    a = xT.reshape(NCH, 128, NT, 512).transpose(2, 1, 0, 3).reshape(NT * 128, NCH * 512)
    return np.ascontiguousarray(a.astype(BF))


def dcol(dram, c, tcol):
    return dram[128 * c:128 * (c + 1), tcol]


def emit_ln(ctx, r, xo, xob, onesb, ones1, srow, pA, pB, rb, sqt):
    nc = ctx.nc
    g, t, v, a = nc.gpsimd, nc.tensor, nc.vector, nc.scalar
    for c in range(NCH):
        v.tensor_copy(chunk(rb, c), chunk(r, c))
    ctx.bar(BAR_DVE_POOL)
    for c in range(NCH):
        g.tensor_mul(chunk(sqt, c), chunk(rb, c), chunk(rb, c))
    ctx.bar()
    for c in range(NCH):
        t.matmul(pA[0:1, :], onesb[:], chunk(rb, c),
                 start=(c == 0), stop=(c == NCH - 1))
    for c in range(NCH):
        t.matmul(pB[0:1, :], onesb[:], chunk(sqt, c),
                 start=(c == 0), stop=(c == NCH - 1))
    ctx.bar()
    v.tensor_scalar_mul(srow[0:1, 512:1024], pA[0:1, :], 1.0 / 1024.0)
    ctx.bar(BAR_DVE_ACT)
    a.activation(srow[0:1, 0:512], srow[0:1, 512:1024], AF.Square)
    ctx.bar(BAR_DVE_ACT)
    v.scalar_tensor_tensor(srow[0:1, 1536:2048], pB[0:1, :], 1.0 / 1024.0,
                           srow[0:1, 0:512], op0=ALU.mult, op1=ALU.subtract)
    ctx.bar(BAR_DVE_ACT)
    ctx.edges["ln_a"](a.activation(srow[0:1, 0:512], srow[0:1, 1536:2048],
                                   AF.Ln, bias=1.0e-5))
    a.activation(srow[0:1, 1024:1536], srow[0:1, 0:512], AF.Exp, scale=-0.5)
    ctx.bar()
    t.matmul(pA[:], ones1[:], srow[0:1, 512:1024], start=True, stop=True)
    t.matmul(pB[:], ones1[:], srow[0:1, 1024:1536], start=True, stop=True)
    ctx.bar()
    for c in range(NCH):
        last = v.tensor_sub(chunk(xo, c), chunk(r, c), pA[:])
    ctx.edges["ln_v"](last)
    ctx.bar()
    for c in range(NCH):
        last = v.tensor_mul(chunk(xo, c), chunk(xo, c), pB[:])
    ctx.edges["ln_v"](last)
    ctx.bar()
    for c in range(NCH):
        a.copy(chunk(xob, c), chunk(xo, c))
    ctx.bar()


def emit_p1(ctx, io, T):
    nc = ctx.nc
    NT = T // 512
    g, sy, t, v, a = nc.gpsimd, nc.sync, nc.tensor, nc.vector, nc.scalar
    ev = ctx.mk_edge(v, "p1_ev")
    ea = ctx.mk_edge(a, "p1_ea")
    ctx.edges["ln_v"] = ev
    ctx.edges["ln_a"] = ea

    wq = ctx.sb("p1_wq", [128, NCH * 3072], BF16)
    wo = ctx.sb("p1_wo", [128, NCH * 1024], BF16)
    wg = ctx.sb("p1_wg", [128, NCH * 16], BF16)
    gam = ctx.sb("p1_gam", [128, NCH], F32)
    bgb = ctx.sb("p1_bgb", [128, 16], F32)
    idn = ctx.sb("p1_idn", [128, 128], BF16)
    onesb = ctx.sb("p1_onesb", [128, 1], BF16)
    ones1 = ctx.sb("p1_ones1", [1, 128], F32)
    bgbb = ctx.sb("p1_bgbb", [128, 16], BF16)
    masks = ctx.sb("p1_masks", [128, 4 * 512], F32)
    maskw = ctx.sb("p1_maskw", [128, 2 * 512], F32)

    xf = ctx.sb("p1_xf", [128, NCH * 512], F32)
    xb = ctx.sb("p1_xb", [128, NCH * 512], BF16)
    sq = ctx.sb("p1_sq", [128, NCH * 512], BF16)
    xn = ctx.sb("p1_xn", [128, NCH * 512], BF16)
    qk = ctx.sb("p1_qk", [128, 16 * 512], BF16)
    kha = ctx.sb("p1_kha", [128, 8 * 256], BF16)
    vtm = ctx.sb("p1_vtm", [128, 4 * 1024], BF16)
    vha = ctx.sb("p1_vha", [128, 2 * 1024], BF16)
    gtm = ctx.sb("p1_gtm", [128, 4 * 16], F32)
    rmss = ctx.sb("p1_rmss", [1, 512], F32)
    smh = ctx.sb("p1_smh", [128, 8 * 512], F32)
    nmx = ctx.sb("p1_nmx", [128, 16], F32)
    rsum = ctx.sb("p1_rsum", [128, 16], F32)
    rrec = ctx.sb("p1_rrec", [128, 16], F32)
    pnh = sq
    pTh = xb
    ao = ctx.sb("p1_ao", [128, NCH * 512], BF16)
    yc = xf
    x1c = ctx.sb("p1_x1c", [128, NCH * 512], F32)
    srow = ctx.sb("p1_srow", [1, 2048], F32)

    pA = ctx.ps("p1_pA", [128, 512], F32)
    pB = ctx.ps("p1_pB", [128, 512], F32)
    pS0 = ctx.ps("p1_pS0", [128, 512], F32)
    pS1 = ctx.ps("p1_pS1", [128, 512], F32)
    pT0 = ctx.ps("p1_pT0", [128, 512], BF16)
    pT1 = ctx.ps("p1_pT1", [128, 512], BF16)
    pO0 = ctx.ps("p1_pO0", [64, 128], F32)
    pO1 = ctx.ps("p1_pO1", [64, 128], F32)

    pairs = []
    for k in range(NCH):
        pairs.append((wq[:, k * 3072:(k + 1) * 3072],
                      io["wqkv"][128 * k:128 * (k + 1), :]))
        pairs.append((wo[:, k * 1024:(k + 1) * 1024],
                      io["wout"][128 * k:128 * (k + 1), :]))
        pairs.append((wg[:, k * 16:(k + 1) * 16],
                      io["wgates"][128 * k:128 * (k + 1), :]))
    pairs.append((bgbb[:], io["bgates_bc"]))
    dma_wait(ctx, g, "p1w", pairs)
    dma_wait(ctx, sy, "p1c", [
        (gam[:], io["gammaC"]), (bgb[:], io["bgates_bc"]),
        (idn[:], io["ident"]), (onesb[:], io["ones128_bf"]),
        (ones1[:], io["ones1_f"]),
        (masks[:, 0:512], io["mask0"]),
        (masks[:, 512:1024], io["mask1"]),
        (masks[:, 1024:1536], io["maskf0"]),
        (masks[:, 1536:2048], io["maskf1"]),
    ])
    g.memset(kha[:], 0.0)
    g.memset(vha[:], 0.0)
    ctx.bar()
    v.tensor_copy(maskw[:, 0:512], masks[:, 1024:1536])
    v.tensor_copy(maskw[:, 512:1024], masks[:, 1536:2048])
    ctx.bar()

    ld = ctx.sem("p1ld")
    stn = ctx.sem("p1st")
    r_ld = sy.alloc_register("r_p1ld")
    r_st = g.alloc_register("r_p1st")
    ctx.pipe("ppA", t, a, precharge=2)
    ctx.pipe("ppD", t, v, precharge=2)
    ctx.pipe("ppS", t, v, precharge=2)
    ctx.pipe("ppT", t, v, precharge=1)
    ctx.pipe("ppV", t, a, precharge=1)

    def tile_body(i):
        tcol = ts(i, 512)
        sy.dma_start(out=xf.ap().rearrange("p (c w) -> p c w", w=512),
                     in_=io["xT"].rearrange("(c p) T -> p c T", p=128)[:, :, tcol]
                     ).then_inc(ld, 16)
        sy.reg_add(r_ld, r_ld, 16)
        sy.wait_ge(ld, r_ld)
        ctx.bar()
        for c in range(NCH):
            v.tensor_copy(chunk(xb, c), chunk(xf, c))
        ctx.bar(BAR_DVE_POOL)
        for c in range(NCH):
            g.tensor_mul(chunk(sq, c), chunk(xb, c), chunk(xb, c))
        ctx.bar()
        for c in range(NCH):
            t.matmul(pA[0:1, :], onesb[:], chunk(sq, c),
                     start=(c == 0), stop=(c == NCH - 1))
        ctx.bar()
        v.tensor_scalar_max(srow[0:1, 0:512], pA[0:1, :], 1.0e-24)
        ctx.bar(BAR_DVE_ACT)
        ea(a.activation(srow[0:1, 512:1024], srow[0:1, 0:512], AF.Ln))
        a.activation(rmss[0:1, :], srow[0:1, 512:1024], AF.Exp, bias=LN32, scale=-0.5)
        ctx.bar()
        t.matmul(pB[:], ones1[:], rmss[0:1, :], start=True, stop=True)
        ctx.bar()
        for c in range(NCH):
            v.scalar_tensor_tensor(chunk(xn, c), chunk(xf, c), gam[:, c:c + 1],
                                   pB[:], op0=ALU.mult, op1=ALU.mult)
        ctx.bar()
        ppA = ctx.pipe("ppA", t, a, precharge=2)
        ppD = ctx.pipe("ppD", t, v, precharge=2)
        for m in range(16):
            bank = pA if m % 2 == 0 else pB
            ppA.before_fill()
            for k in range(NCH):
                mm = t.matmul(bank[:], wq[:, k * 3072 + m * 128:k * 3072 + (m + 1) * 128],
                              chunk(xn, k), start=(k == 0), stop=(k == NCH - 1))
            ppA.after_fill(mm)
            ppA.before_drain()
            ppA.after_drain(a.copy(chunk(qk, m), bank[:]))
        ctx.bar(BAR_PE_ACT)
        for tc in range(4):
            for half in range(2):
                bank = pA if half == 0 else pB
                ppD.before_fill()
                for k in range(NCH):
                    mm = t.matmul(bank[:],
                                  xn[:, k * 512 + tc * 128:k * 512 + (tc + 1) * 128],
                                  wq[:, k * 3072 + 2048 + half * 512:
                                        k * 3072 + 2048 + (half + 1) * 512],
                                  start=(k == 0), stop=(k == NCH - 1))
                ppD.after_fill(mm)
                ppD.before_drain()
                ppD.after_drain(v.tensor_copy(
                    vtm[:, tc * 1024 + half * 512:tc * 1024 + (half + 1) * 512],
                    bank[:]))
        ctx.bar(BAR_PE_DVE)
        for tc in range(4):
            bank = pA if tc % 2 == 0 else pB
            ppA.before_fill()
            for k in range(NCH):
                t.matmul(bank[0:128, 0:16],
                         xn[:, k * 512 + tc * 128:k * 512 + (tc + 1) * 128],
                         wg[:, k * 16:(k + 1) * 16],
                         start=(k == 0), stop=False)
            mm = t.matmul(bank[0:128, 0:16], idn[:], bgbb[:],
                          start=False, stop=True)
            ppA.after_fill(mm)
            ppA.before_drain()
            ppA.after_drain(a.activation(gtm[:, tc * 16:(tc + 1) * 16],
                                         bank[0:128, 0:16], AF.Sigmoid))
        ctx.bar()
        for b in range(2):
            for qc in range(2):
                tc = 2 * b + qc
                moff = qc * 512
                for gp in range(2):
                    hh0 = 4 * gp
                    ppS = ctx.pipe("ppS", t, v, precharge=2)
                    for hh in range(hh0, hh0 + 4):
                        pair = (pS0, pS1) if hh % 2 == 0 else (pA, pB)
                        ppS.before_fill()
                        for r in range(2):
                            scr = pair[r]
                            lhs = qk[r * 64:(r + 1) * 64,
                                     hh * 512 + b * 256 + qc * 128:
                                     hh * 512 + b * 256 + qc * 128 + 128]
                            if b == 0:
                                prev = kha[r * 64:(r + 1) * 64, hh * 256:(hh + 1) * 256]
                            else:
                                prev = qk[r * 64:(r + 1) * 64,
                                          (8 + hh) * 512:(8 + hh) * 512 + 256]
                            cur = qk[r * 64:(r + 1) * 64,
                                     (8 + hh) * 512 + b * 256:(8 + hh) * 512 + (b + 1) * 256]
                            t.matmul(scr[:, 0:256], lhs, prev, start=True, stop=True)
                            mm = t.matmul(scr[:, 256:512], lhs, cur, start=True, stop=True)
                        ppS.after_fill(mm)
                        ppS.before_drain()
                        for r in range(2):
                            scr = pair[r]
                            lh = 2 * (hh - hh0) + r
                            msrc = maskw if b == 0 else masks
                            last = v.tensor_add(chunk(smh, lh), scr[:],
                                                msrc[:, moff:moff + 512])
                        ppS.after_drain(last)
                    ev(v.tensor_copy(smh[:, 0:1], smh[:, 0:1]))
                    for lh in range(8):
                        last = v.reduce_max(nmx[:, 8 * gp + lh:8 * gp + lh + 1],
                                            chunk(smh, lh), axis=AX.X, negate=True)
                    ev(last)
                    ev(v.tensor_scalar_mul(nmx[:, 8 * gp:8 * gp + 8],
                                           nmx[:, 8 * gp:8 * gp + 8], 0.125))
                    ctx.bar(BAR_DVE_ACT)
                    for lh in range(8):
                        last = a.activation(chunk(pnh, lh), chunk(smh, lh), AF.Exp,
                                            bias=nmx[:, 8 * gp + lh:8 * gp + lh + 1],
                                            scale=0.125,
                                            accum_out=rsum[:, 8 * gp + lh:
                                                           8 * gp + lh + 1])
                    ea(last)
                    ctx.bar(BAR_DVE_ACT)
                    ev(v.reciprocal(rrec[:, 8 * gp:8 * gp + 8],
                                    rsum[:, 8 * gp:8 * gp + 8]))
                    ctx.bar(BAR_DVE_ACT)
                    for lh in range(8):
                        h = 8 * gp + lh
                        last = v.tensor_scalar(chunk(pnh, lh), chunk(pnh, lh),
                                                rrec[:, h:h + 1],
                                                gtm[:, tc * 16 + h:tc * 16 + h + 1],
                                                op0=ALU.mult, op1=ALU.mult)
                    ev(last)
                    ctx.bar()
                    ppT = ctx.pipe("ppT", t, v, precharge=1)
                    for hh in range(hh0, hh0 + 4):
                        ppT.before_fill()
                        for r in range(2):
                            lh = 2 * (hh - hh0) + r
                            ptr = pT0 if r == 0 else pT1
                            for jc in range(4):
                                mm = t.transpose(ptr[:, jc * 128:(jc + 1) * 128],
                                                 pnh[:, lh * 512 + jc * 128:
                                                     lh * 512 + (jc + 1) * 128],
                                                 idn[:])
                        ppT.after_fill(mm)
                        ppT.before_drain()
                        for r in range(2):
                            lh = 2 * (hh - hh0) + r
                            ptr = pT0 if r == 0 else pT1
                            last = v.tensor_copy(chunk(pTh, lh), ptr[:])
                        ppT.after_drain(last)
                    ctx.bar(BAR_PE_DVE)
                    ppV = ctx.pipe("ppV", t, a, precharge=1)
                    for hh in range(hh0, hh0 + 4):
                        ppV.before_fill()
                        for r in range(2):
                            h = 2 * hh + r
                            lh = 2 * (hh - hh0) + r
                            po = pO0 if r == 0 else pO1
                            for jc in range(4):
                                if b == 0:
                                    if jc < 2:
                                        vsl = vha[:, jc * 1024 + h * 64:
                                                  jc * 1024 + h * 64 + 64]
                                    else:
                                        vsl = vtm[:, (jc - 2) * 1024 + h * 64:
                                                  (jc - 2) * 1024 + h * 64 + 64]
                                else:
                                    vsl = vtm[:, jc * 1024 + h * 64:
                                              jc * 1024 + h * 64 + 64]
                                mm = t.matmul(po[:], vsl,
                                              pTh[:, lh * 512 + jc * 128:
                                                  lh * 512 + (jc + 1) * 128],
                                              start=(jc == 0), stop=(jc == 3))
                        ppV.after_fill(mm)
                        ppV.before_drain()
                        for r in range(2):
                            po = pO0 if r == 0 else pO1
                            last = a.copy(ao[r * 64:(r + 1) * 64,
                                             hh * 512 + tc * 128:hh * 512 + (tc + 1) * 128],
                                          po[:])
                        ppV.after_drain(last)
        ctx.bar()
        v.tensor_copy(
            kha.ap().rearrange("p (h c) -> p h c", c=256),
            qk.ap()[:, 8 * 512:].rearrange("p (h c) -> p h c", c=512)[:, :, 256:512])
        v.tensor_copy(vha[:], vtm[:, 2 * 1024:4 * 1024])
        v.tensor_copy(maskw[:], masks[:, 0:1024])
        ctx.bar()
        for c in range(NCH):
            bank = pA if c % 2 == 0 else pB
            ppD.before_fill()
            for k in range(NCH):
                mm = t.matmul(bank[:], wo[:, k * 1024 + c * 128:k * 1024 + (c + 1) * 128],
                              chunk(ao, k), start=(k == 0), stop=(k == NCH - 1))
            ppD.after_fill(mm)
            ppD.before_drain()
            ppD.after_drain(v.tensor_add(chunk(xf, c), bank[:], chunk(xf, c)))
        ctx.bar()
        emit_ln(ctx, xf, x1c, xn, onesb, ones1, srow, pA, pB, xb, sq)
        g.dma_start(out=io["x1T"].rearrange("(c p) T -> p c T", p=128)[:, :, tcol],
                    in_=x1c.ap().rearrange("p (c w) -> p c w", w=512)).then_inc(stn, 16)
        g.dma_start(out=io["x1bT"][ts(i, 128)], in_=xn[:]).then_inc(stn, 16)
        g.reg_add(r_st, r_st, 32)
        g.wait_ge(stn, r_st)
        ctx.bar()

    sy.reg_mov(r_ld, ctx.counts["p1ld"])
    g.reg_mov(r_st, ctx.counts["p1st"])
    ctx.bar()
    with nc.Fori(0, NT) as i:
        tile_body(i)
    ctx.counts["p1ld"] += 16 * NT
    ctx.counts["p1st"] += 32 * NT
    ctx.bar()


def build_p1(T):
    nc = bacc.Bacc("TRN2", target_bir_lowering=False, num_devices=8)
    io = {}
    io["xT"] = nc.dram_tensor("xT", [D, T], F32, kind="ExternalInput").ap()
    io["wqkv"] = nc.dram_tensor("wqkv", [D, 3072], F32, kind="ExternalInput").ap()
    io["wgates"] = nc.dram_tensor("wgates", [D, 16], F32, kind="ExternalInput").ap()
    io["wout"] = nc.dram_tensor("wout", [D, D], F32, kind="ExternalInput").ap()
    for nm, shp, dt in [("gammaC", [128, NCH], F32), ("bgates_bc", [128, 16], F32),
                        ("ident", [128, 128], BF16), ("ones128_bf", [128, 1], BF16),
                        ("ones1_f", [1, 128], F32), ("ones1_bf", [1, 128], BF16),
                        ("bgates_row_bf", [1, 16], BF16), ("mask0", [128, 512], F32),
                        ("mask1", [128, 512], F32), ("maskf0", [128, 512], F32),
                        ("maskf1", [128, 512], F32)]:
        io[nm] = nc.dram_tensor(nm, shp, dt, kind="ExternalInput").ap()
    io["fconsts"] = nc.dram_tensor("fconsts", [1, 64], F32, kind="ExternalInput").ap()
    io["x1T"] = nc.dram_tensor("x1T", [D, T], F32, kind="ExternalOutput").ap()
    io["x1bT"] = nc.dram_tensor("x1bT", [T // 512 * 128, NCH * 512], BF16,
                                kind="ExternalOutput").ap()
    with ExitStack() as stack:
        ctx = Ctx(nc, stack)
        ctx.setup_consts(io["fconsts"])
        emit_p1(ctx, io, T)
    nc.finalize()
    return nc


def p1_inputs(x_core, inp, consts):
    m = {
        "xT": np.ascontiguousarray(x_core.T),
        "wqkv": inp["L0_attn_W_qkv"],
        "wgates": inp["L0_attn_W_gates"],
        "wout": inp["L0_attn_W_out"],
        "gammaC": np.ascontiguousarray(
            np.asarray(inp["L0_attn_gamma"], np.float32).reshape(NCH, 128).T),
        "bgates_bc": np.broadcast_to(
            np.asarray(inp["L0_attn_b_gates"], np.float32), (128, 16)).copy(),
        "ones1_bf": np.ones((1, 128), BF),
        "bgates_row_bf": np.asarray(inp["L0_attn_b_gates"], np.float32
                                    ).reshape(1, 16).astype(BF),
    }
    for k in ("ident", "ones128_bf", "ones1_f", "mask0", "mask1", "maskf0",
              "maskf1", "fconsts"):
        m[k] = consts[k]
    return {k: np.ascontiguousarray(np.asarray(v)) for k, v in m.items()}


def _run(nc, maps):
    return run_bass_kernel_spmd(nc, maps, core_ids=list(range(NCORES))).results


NCORES = 8


def kernel_multi(**inputs):
    T = inputs["L0_attn_x"].shape[1]
    N = NCORES * T
    cap = max(int(math.ceil(math.ceil(N * 2 / 8) * 1.25)), 1)
    consts = host_constants()
    x = np.asarray(inputs["L0_attn_x"], np.float32)

    nc = build_p1(T)
    r1 = _run(nc, [p1_inputs(x[c], inputs, consts) for c in range(NCORES)])

    nc = build_p2a(T)
    r2a = _run(nc, [p2a_inputs(r1[c]["x1T"], inputs, consts) for c in range(NCORES)])
    W8all = np.concatenate([r2a[c]["W8L"] for c in range(NCORES)], axis=0)

    nc = build_p2b(T, cap)
    r2b = _run(nc, [p2b_inputs(W8all, r2a[c]["Atm"], consts, percore_constants(c))
                    for c in range(NCORES)])

    nc = build_p2c(T)
    r2c = _run(nc, [p2c_inputs(r1[c]["x1T"], r1[c]["x1bT"], r2b[c]["coefT"],
                               inputs, consts) for c in range(NCORES)])

    nc = build_p3(T)
    r3 = _run(nc, [p3_inputs(r2c[c]["x2T"], r2c[c]["x2bT"], inputs, consts)
                   for c in range(NCORES)])

    nc = build_p4(T)
    r4 = _run(nc, [p4_inputs(r3[c]["x3T"], r3[c]["x3bT"], inputs, consts)
                   for c in range(NCORES)])

    return np.stack([np.ascontiguousarray(r4[c]["x4T"].T) for c in range(NCORES)],
                    axis=0)


def emit_p2a(ctx, io, T):
    nc = ctx.nc
    NT = T // 512
    NC = T // 128
    g, sy, t, v, a = nc.gpsimd, nc.sync, nc.tensor, nc.vector, nc.scalar
    ev = ctx.mk_edge(v, "p2a_ev")

    pw = ctx.sb("p2a_pw", [128, NCH * 8], F32)
    pbb = ctx.sb("p2a_pbb", [128, 8], F32)
    ebb = ctx.sb("p2a_ebb", [128, 8], F32)
    idnf = ctx.sb("p2a_idnf", [128, 128], F32)
    x1f = ctx.sb("p2a_x1f", [128, NCH * 512], F32)
    cv = ctx.sb("p2a_cv", [128, 8], F32)
    sc1 = ctx.sb("p2a_sc1", [128, 8], F32)
    sc2 = ctx.sb("p2a_sc2", [128, 8], F32)
    sc3 = ctx.sb("p2a_sc3", [128, 8], F32)
    oh1 = ctx.sb("p2a_oh1", [128, 8], F32)
    oh2 = ctx.sb("p2a_oh2", [128, 8], F32)
    m1 = ctx.sb("p2a_m1", [128, 1], F32)
    m2 = ctx.sb("p2a_m2", [128, 1], F32)
    w1 = ctx.sb("p2a_w1", [128, 1], F32)
    w2 = ctx.sb("p2a_w2", [128, 1], F32)
    dsum = ctx.sb("p2a_d", [128, 1], F32)
    atm = ctx.sb("p2a_atm", [128, NC * 16], F32)
    w8l = ctx.sb("p2a_w8l", [8, 2 * T], F32)
    pZ = ctx.ps("p2a_pZ", [128, 8], F32)
    pTr = ctx.ps("p2a_pTr", [8, 128], F32)

    pairs = [(pw[:, k * 8:(k + 1) * 8], io["projwT"][128 * k:128 * (k + 1), :])
             for k in range(NCH)]
    dma_wait(ctx, g, "p2aw", pairs)
    dma_wait(ctx, sy, "p2ac", [(pbb[:], io["projb_bc"]),
                               (ebb[:], io["ebias_bc"]),
                               (idnf[:], io["identf"])])
    ctx.bar()

    ld = ctx.sem("p2ald")
    for ti in range(NT):
        sy.dma_start(out=x1f.ap().rearrange("p (c w) -> p c w", w=512),
                     in_=io["x1T"].rearrange("(c p) T -> p c T", p=128)
                     [:, :, ti * 512:(ti + 1) * 512]).then_inc(ld, 16)
        ctx.counts["p2ald"] += 16
        sy.wait_ge(ld, ctx.counts["p2ald"])
        ctx.bar()
        for tc4 in range(4):
            cglob = ti * 4 + tc4
            for k in range(NCH):
                t.matmul(pZ[:], x1f[:, k * 512 + tc4 * 128:k * 512 + (tc4 + 1) * 128],
                         pw[:, k * 8:(k + 1) * 8], start=(k == 0), stop=(k == NCH - 1))
            ctx.bar(BAR_PE_DVE_ACT)
            ev(v.tensor_add(sc1[:], pZ[:], pbb[:]))
            ctx.bar(BAR_PE_DVE_ACT)
            a.activation(sc2[:], sc1[:], AF.Sigmoid)
            ctx.bar(BAR_DVE_ACT)
            ev(v.tensor_add(sc2[:], sc2[:], ebb[:]))
            ev(v.tensor_scalar_max(cv[:], sc2[:], 1.0e-12))
            ev(v.reduce_max(m1[:], cv[:], axis=AX.X))
            ev(v.tensor_scalar(sc1[:], cv[:], m1[:], None, op0=ALU.is_equal))
            ev(v.tensor_tensor_scan(sc2[:], sc1[:], sc1[:], 0.0,
                                    op0=ALU.add, op1=ALU.bypass))
            ev(v.tensor_scalar(sc3[:], sc2[:], 1.0, None, op0=ALU.is_equal))
            ev(v.tensor_mul(oh1[:], sc1[:], sc3[:]))
            ev(v.scalar_tensor_tensor(sc3[:], oh1[:], -1.0e30, cv[:],
                                      op0=ALU.mult, op1=ALU.add))
            ev(v.reduce_max(m2[:], sc3[:], axis=AX.X))
            ev(v.tensor_scalar(sc1[:], sc3[:], m2[:], None, op0=ALU.is_equal))
            ev(v.tensor_tensor_scan(sc2[:], sc1[:], sc1[:], 0.0,
                                    op0=ALU.add, op1=ALU.bypass))
            ev(v.tensor_scalar(sc3[:], sc2[:], 1.0, None, op0=ALU.is_equal))
            ev(v.tensor_mul(oh2[:], sc1[:], sc3[:]))
            ev(v.tensor_add(dsum[:], m1[:], m2[:]))
            ev(v.tensor_scalar_max(dsum[:], dsum[:], 1.0e-12))
            ev(v.reciprocal(dsum[:], dsum[:]))
            ev(v.tensor_mul(w1[:], m1[:], dsum[:]))
            ev(v.tensor_mul(w2[:], m2[:], dsum[:]))
            ev(v.tensor_scalar_add(m1[:], w1[:], 1.0))
            ev(v.tensor_scalar_add(m2[:], w2[:], 1.0))
            ev(v.tensor_scalar(atm[:, cglob * 16:cglob * 16 + 8], oh1[:], m1[:],
                               -1.0, op0=ALU.mult, op1=ALU.add))
            ev(v.tensor_scalar(atm[:, cglob * 16 + 8:cglob * 16 + 16], oh2[:], m2[:],
                               -1.0, op0=ALU.mult, op1=ALU.add))
            ctx.bar()
            for k in range(2):
                t.transpose(pTr[:], atm[:, cglob * 16 + k * 8:cglob * 16 + (k + 1) * 8],
                            idnf[:])
                ctx.bar(BAR_PE_DVE)
                v.tensor_copy(w8l[:, cglob * 256 + k * 128:cglob * 256 + (k + 1) * 128],
                              pTr[:])
                ctx.bar(BAR_PE_DVE)
    ctx.bar()
    dma_wait(ctx, sy, "p2ao", [(io["W8L"], w8l[:]), (io["Atm"], atm[:])])
    ctx.bar()


def emit_p2b(ctx, io, T, cap, iters=40):
    nc = ctx.nc
    NC = T // 128
    g, sy, t, v, a = nc.gpsimd, nc.sync, nc.tensor, nc.vector, nc.scalar
    ev = ctx.mk_edge(v, "p2b_ev")

    GP = 16 * T // 1024
    NF = 2 * T * 8 // 128
    w8e = ctx.sb("p2b_w8e", [128, 8 * NF], F32)
    atm = ctx.sb("p2b_atm", [128, NC * 16], F32)
    lo8 = ctx.sb("p2b_lo8", [1, 8], F32)
    hi8 = ctx.sb("p2b_hi8", [1, 8], F32)
    mid8 = ctx.sb("p2b_mid8", [1, 8], F32)
    sel8 = ctx.sb("p2b_sel8", [1, 8], F32)
    tmp8 = ctx.sb("p2b_tmp8", [1, 8], F32)
    r8 = ctx.sb("p2b_r8", [1, 8], F32)
    offs = ctx.sb("p2b_offs", [1, 8], F32)
    runoff = ctx.sb("p2b_runoff", [1, 8], F32)
    t8b = ctx.sb("p2b_t8b", [128, 8], F32)
    t8lo = ctx.sb("p2b_t8lo", [128, 8], F32)
    t8hi = ctx.sb("p2b_t8hi", [128, 8], F32)
    r8b = ctx.sb("p2b_r8b", [128, 8], F32)
    offb = ctx.sb("p2b_offb", [128, 8], F32)
    runb = ctx.sb("p2b_runb", [128, 8], F32)
    rowc = ctx.sb("p2b_rowc", [128, 8], F32)
    rowt = ctx.sb("p2b_rowt", [128, 8], F32)
    cmpo = ctx.sb("p2b_cmpo", [128, NF], BF16)
    cmp2 = ctx.sb("p2b_cmp2", [128, NF], BF16)
    cnt88 = ctx.sb("p2b_cnt88", [8, 8], F32)
    onesf = ctx.sb("p2b_onesf", [128, 1], F32)
    ones1 = ctx.sb("p2b_ones1", [1, 128], F32)
    ltri = ctx.sb("p2b_ltri", [128, 128], F32)
    cmask = ctx.sb("p2b_cmask", [128, 8], F32)
    pmask = ctx.sb("p2b_pmask", [8, 1], F32)
    idnf = ctx.sb("p2b_idnf", [128, 128], F32)
    tie0 = ctx.sb("p2b_tie0", [128, 8], F32)
    tie1 = ctx.sb("p2b_tie1", [128, 8], F32)
    st0 = ctx.sb("p2b_st0", [128, 8], F32)
    st1 = ctx.sb("p2b_st1", [128, 8], F32)
    rnk = ctx.sb("p2b_rnk", [128, 8], F32)
    scr = ctx.sb("p2b_scr", [128, 8], F32)
    scr2 = ctx.sb("p2b_scr2", [128, 8], F32)
    wk = ctx.sb("p2b_wk", [128, 1], F32)
    ctm = ctx.sb("p2b_ctm", [128, 8], F32)
    c8sb = ctx.sb("p2b_c8sb", [8, T], F32)
    pC = ctx.ps("p2b_pC", [128, 8], F32)
    pD = ctx.ps("p2b_pD", [128, 128], F32)
    pT8 = ctx.ps("p2b_pT8", [1, 8], F32)

    prs = []
    for e in range(8):
        src = io["W8all"].rearrange("(r e) (p f) -> e r p f", e=8, p=16)[e]
        prs.append((w8e[:, e * NF:(e + 1) * NF], src))
    dma_wait(ctx, sy, "p2bw", prs)
    dma_wait(ctx, g, "p2bc", [
        (atm[:], io["Atm"]), (onesf[:], io["onesf_col"]),
        (ones1[:], io["ones1_f"]), (ltri[:], io["ltri_f"]),
        (cmask[:], io["coremask"]), (pmask[:], io["prevmask"]),
        (idnf[:], io["identf"]),
    ])
    g.memset(lo8[:], -1.0)
    g.memset(hi8[:], 2.0)
    g.memset(runoff[:], 0.0)
    g.memset(runb[:], 0.0)
    ctx.bar()

    def count_pass(thr_row, strict_gt, rowout):
        t.matmul(pT8[:], ones1[0:1, 0:1], thr_row[0:1, :], start=True, stop=True)
        t.matmul(pC[:], ones1[:], thr_row[0:1, :], start=True, stop=True)
        ctx.bar(BAR_PE_DVE)
        ev(v.tensor_copy(t8b[:], pC[:]))
        ctx.bar(BAR_PE_DVE)
        for e in range(8):
            op = ALU.is_gt if strict_gt else ALU.is_le
            ev(v.tensor_scalar(cmpo[:], w8e[:, e * NF:(e + 1) * NF],
                               t8b[:, e:e + 1], None, op0=op, op1=ALU.add,
                               accum_out=rowout[:, e:e + 1]))
        ctx.bar()

    for it in range(iters):
        ev(v.tensor_add(mid8[:], lo8[:], hi8[:]))
        ev(v.tensor_scalar_mul(mid8[:], mid8[:], 0.5))
        ctx.bar()
        count_pass(mid8, True, rowc)
        t.matmul(pT8[0:1, :], onesf[:], rowc[:], start=True, stop=True)
        ctx.bar(BAR_PE_DVE)
        ev(v.tensor_scalar(sel8[:], pT8[0:1, :], float(cap), None, op0=ALU.is_ge))
        ev(v.tensor_sub(tmp8[:], mid8[:], lo8[:]))
        ev(v.tensor_mul(tmp8[:], tmp8[:], sel8[:]))
        ev(v.tensor_add(lo8[:], lo8[:], tmp8[:]))
        ev(v.tensor_scalar(sel8[:], sel8[:], -1.0, 1.0, op0=ALU.mult, op1=ALU.add))
        ev(v.tensor_sub(tmp8[:], mid8[:], hi8[:]))
        ev(v.tensor_mul(tmp8[:], tmp8[:], sel8[:]))
        ev(v.tensor_add(hi8[:], hi8[:], tmp8[:]))
        ctx.bar()

    count_pass(hi8, True, rowc)
    ev(v.tensor_copy(t8hi[:], t8b[:]))
    t.matmul(pT8[0:1, :], onesf[:], rowc[:], start=True, stop=True)
    ctx.bar()
    ev(v.tensor_scalar(r8[:], pT8[0:1, :], -1.0, float(cap),
                       op0=ALU.mult, op1=ALU.add))
    ctx.bar()
    t.matmul(pC[:], ones1[:], lo8[0:1, :], start=True, stop=True)
    ctx.bar(BAR_PE_DVE)
    ev(v.tensor_copy(t8lo[:], pC[:]))
    ctx.bar()
    for e in range(8):
        ev(v.tensor_scalar(cmpo[:], w8e[:, e * NF:(e + 1) * NF],
                           t8lo[:, e:e + 1], None, op0=ALU.is_gt))
        ev(v.tensor_scalar(cmp2[:], w8e[:, e * NF:(e + 1) * NF],
                           t8hi[:, e:e + 1], None, op0=ALU.is_le))
        ev(v.tensor_tensor(cmpo[:], cmpo[:], cmp2[:], op=ALU.mult))
        ev(v.tensor_scalar(cmp2[:], cmpo[:], 1.0, None, op0=ALU.mult, op1=ALU.add,
                           accum_out=rowt[:, e:e + 1]))
    ctx.bar()
    t.matmul(pC[0:8, :], cmask[:], rowt[:], start=True, stop=True)
    ctx.bar(BAR_PE_DVE)
    ev(v.tensor_copy(cnt88[:], pC[0:8, :]))
    ctx.bar()
    t.matmul(pT8[0:1, :], pmask[:], cnt88[:], start=True, stop=True)
    ctx.bar(BAR_PE_DVE)
    ev(v.tensor_copy(offs[:], pT8[0:1, :]))
    ctx.bar()
    t.matmul(pC[:], ones1[:], r8[0:1, :], start=True, stop=True)
    ctx.bar(BAR_PE_DVE)
    ev(v.tensor_copy(r8b[:], pC[:]))
    ctx.bar(BAR_PE_DVE)
    t.matmul(pC[:], ones1[:], offs[0:1, :], start=True, stop=True)
    ctx.bar(BAR_PE_DVE)
    ev(v.tensor_copy(offb[:], pC[:]))
    ctx.bar()

    for c in range(NC):
        A0 = atm[:, c * 16:c * 16 + 8]
        A1 = atm[:, c * 16 + 8:c * 16 + 16]
        ev(v.tensor_tensor(st0[:], A0, t8hi[:], op=ALU.is_gt))
        ev(v.tensor_tensor(st1[:], A1, t8hi[:], op=ALU.is_gt))
        ev(v.tensor_tensor(tie0[:], A0, t8lo[:], op=ALU.is_gt))
        ev(v.tensor_tensor(scr[:], A0, t8hi[:], op=ALU.is_le))
        ev(v.tensor_mul(tie0[:], tie0[:], scr[:]))
        ev(v.tensor_tensor(tie1[:], A1, t8lo[:], op=ALU.is_gt))
        ev(v.tensor_tensor(scr[:], A1, t8hi[:], op=ALU.is_le))
        ev(v.tensor_mul(tie1[:], tie1[:], scr[:]))
        ev(v.tensor_add(scr[:], tie0[:], tie1[:]))
        ctx.bar()
        t.matmul(pD[:, 0:8], ltri[:], scr[:], start=True, stop=True)
        t.matmul(pT8[0:1, :], onesf[:], scr[:], start=True, stop=True)
        ctx.bar()
        for k in range(2):
            tie = tie0 if k == 0 else tie1
            stk = st0 if k == 0 else st1
            Ak = A0 if k == 0 else A1
            ev(v.tensor_add(rnk[:], runb[:], pD[:, 0:8]))
            if k == 1:
                ev(v.tensor_add(rnk[:], rnk[:], tie0[:]))
            ev(v.tensor_add(rnk[:], rnk[:], offb[:]))
            ev(v.tensor_tensor(scr2[:], rnk[:], r8b[:], op=ALU.is_lt))
            ev(v.tensor_mul(scr2[:], scr2[:], tie[:]))
            ev(v.tensor_add(scr2[:], scr2[:], stk[:]))
            ev(v.reduce_max(wk[:], Ak, axis=AX.X))
            ev(v.tensor_scalar(scr2[:], scr2[:], wk[:], None, op0=ALU.mult))
            if k == 0:
                ev(v.tensor_copy(ctm[:], scr2[:]))
            else:
                ev(v.tensor_add(ctm[:], ctm[:], scr2[:]))
        ev(v.tensor_add(runoff[:], runoff[:], pT8[0:1, :]))
        ctx.bar()
        t.matmul(pC[:], ones1[:], runoff[0:1, :], start=True, stop=True)
        t.transpose(pD[0:8, :], ctm[:], idnf[:])
        ctx.bar(BAR_PE_DVE)
        ev(v.tensor_copy(runb[:], pC[:]))
        ev(v.tensor_copy(c8sb[:, c * 128:(c + 1) * 128], pD[0:8, :]))
        ctx.bar()
    ctx.bar()
    dma_wait(ctx, sy, "p2bo", [(io["coefT"], c8sb[:])])
    ctx.bar()


def emit_p2c(ctx, io, T):
    nc = ctx.nc
    NT = T // 512
    g, sy, t, v, a = nc.gpsimd, nc.sync, nc.tensor, nc.vector, nc.scalar
    ev = ctx.mk_edge(v, "p2c_ev")

    onesb = ctx.sb("p2c_onesb", [128, 1], BF16)
    ones1 = ctx.sb("p2c_ones1", [1, 128], F32)
    oh8 = ctx.sb("p2c_oh8", [8, 8 * 128], BF16)
    c8sb = ctx.sb("p2c_c8", [8, T], BF16)
    x1b = ctx.sb("p2c_x1b", [128, 2 * NCH * 512], BF16)
    ysb = ctx.sb("p2c_ysb", [128, 2 * NCH * 512], F32)
    pA = ctx.ps("p2c_pA", [128, 512], F32)
    pB = ctx.ps("p2c_pB", [128, 512], F32)
    pC = ctx.ps("p2c_pC", [128, 512], F32)

    dma_wait(ctx, sy, "p2cc", [(onesb[:], io["ones128_bf"]),
                               (ones1[:], io["ones1_f"]),
                               (oh8[:], io["oh8"])])
    dma_wait(ctx, g, "p2cc2", [(c8sb[:], io["coefT"])])
    ctx.bar()

    ld = ctx.sem("p2cld")
    stn = ctx.sem("p2cst")
    r_ld = sy.alloc_register("r_ld2c")
    r_st = g.alloc_register("r_st2c")
    ppA = ctx.pipe("ppA", t, a, precharge=2)
    ppD = ctx.pipe("ppD", t, v, precharge=2)
    ppC = ctx.pipe("ppC", t, v, precharge=1)
    ppM = ctx.pipe("ppM", a, v, precharge=1)

    NTc = T // 512

    def issue_x1b(i):
        par = i % 2
        sy.dma_start(out=x1b.ap()[:, ts(par, NCH * 512)],
                     in_=io["x1bT"][ts(i, 128)]).then_inc(ld, 16)

    def load_x1b(i):
        sy.reg_add(r_ld, r_ld, 16)
        sy.wait_ge(ld, r_ld)
        issue_x1b((i + 1) % NTc)

    def xchunk(i, c):
        par = i % 2
        return x1b.ap()[:, ts(par, NCH * 512)][:, c * 512:(c + 1) * 512]

    def ychunk(i, m):
        par = i % 2
        return ysb.ap()[:, ts(par, NCH * 512)][:, m * 512:(m + 1) * 512]

    r_stw = g.alloc_register("r_stw2c")

    s1 = ctx.sb("p2c_s1", [128, NCH * 2048], BF16)
    s2 = ctx.sb("p2c_s2", [128, 16 * 1024], BF16)
    s1b = ctx.sb("p2c_s1b", [128, 16], F32)
    s2b = ctx.sb("p2c_s2b", [128, 8], F32)
    f2b = ctx.sb("p2c_f2b", [8, 1024], BF16)
    hsb = ctx.sb("p2c_hsb", [128, 16 * 512], BF16)
    for half in range(2):
        prs = []
        for k in range(NCH):
            prs.append((s1[:, k * 2048:(k + 1) * 2048],
                        io["sh1T"][128 * k:128 * (k + 1),
                                   half * 2048:(half + 1) * 2048]))
        for k in range(16):
            prs.append((s2[:, k * 1024:(k + 1) * 1024],
                        io["sh2T"][half * 2048 + 128 * k:half * 2048 + 128 * (k + 1), :]))
        dma_wait(ctx, g, f"p2cw{half}", prs)
        dma_wait(ctx, sy, f"p2cb{half}", [
            (s1b[:], io["sh1bC"][:, half * 16:(half + 1) * 16]),
            (s2b[:], io["sh2bC"]),
        ])
        dma_wait(ctx, g, f"p2cb2{half}", [(f2b[:], io["fc2b8"])])
        sy.reg_mov(r_ld, ctx.counts["p2cld"])
        g.reg_mov(r_st, ctx.counts["p2cst"])
        g.reg_mov(r_stw, ctx.counts["p2cst"])
        issue_x1b(0)
        ctx.counts["p2cld"] += 16
        ctx.bar()
        with nc.Fori(0, NT) as i:
            load_x1b(i)
            g.wait_ge(stn, r_stw)
            ctx.bar()
            for m in range(16):
                bank = pA if m % 2 == 0 else pB
                ppA.before_fill()
                for k in range(NCH):
                    mm = t.matmul(bank[:], s1[:, k * 2048 + m * 128:k * 2048 + (m + 1) * 128],
                                  xchunk(i, k), start=(k == 0), stop=(k == NCH - 1))
                ppA.after_fill(mm)
                ppA.before_drain()
                ppA.after_drain(a.activation(chunk(hsb, m), bank[:], AF.Gelu,
                                             bias=s1b[:, m:m + 1]))
            ctx.bar()
            for m in range(NCH):
                bank = pA if m % 2 == 0 else pB
                ppD.before_fill()
                for k in range(16):
                    mm = t.matmul(bank[:], s2[:, k * 1024 + m * 128:k * 1024 + (m + 1) * 128],
                                  chunk(hsb, k), start=(k == 0), stop=(k == 16 - 1))
                ppD.after_fill(mm)
                if half == 0:
                    ppC.before_fill()
                    ppC.after_fill(t.matmul(pC[:], f2b[:, m * 128:(m + 1) * 128],
                                            c8sb[:, ts(i, 512)], start=True, stop=True))
                ppD.before_drain()
                if half == 0:
                    op1 = v.tensor_scalar_add(ychunk(i, m), bank[:], s2b[:, m:m + 1])
                    ppD.after_drain(op1)
                    ev(v.tensor_copy(s2b[:, 0:1], s2b[:, 0:1]))
                    ppC.before_drain()
                    ppC.after_drain(v.tensor_add(ychunk(i, m), ychunk(i, m), pC[:]))
                else:
                    ppD.after_drain(v.tensor_copy(ychunk(i, m), bank[:]))
            ctx.bar()
            ydst = io["ydram"].rearrange("(c p) T -> p c T", p=128)[:, :, ts(i, 512)]
            ysrc = ysb.ap()[:, ts(i % 2, NCH * 512)].rearrange("p (c w) -> p c w", w=512)
            if half == 0:
                g.dma_start(out=ydst, in_=ysrc).then_inc(stn, 16)
            else:
                g.dma_start(out=ydst, in_=ysrc, accum_op=ALU.add).then_inc(stn, 16)
            g.reg_add(r_st, r_st, 16)
            g.reg_add(r_stw, r_stw, 16)
            ctx.bar()
        g.wait_ge(stn, r_st)
        ctx.counts["p2cld"] += 16 * NT
        ctx.counts["p2cst"] += 16 * NT
        ctx.bar()

    w1e = s1
    w2e = s2
    f1b = s1b
    heb = hsb
    cbs = ctx.sb("p2c_cbs", [128, 512], BF16)
    for e in range(8):
        prs = [(w1e[:, k * 2048:(k + 1) * 2048],
                io["fc1T"][e * 1024 + 128 * k:e * 1024 + 128 * (k + 1), :])
               for k in range(NCH)]
        prs += [(w2e[:, k * 1024:(k + 1) * 1024],
                 io["fc2T"][e * 2048 + 128 * k:e * 2048 + 128 * (k + 1), :])
                for k in range(16)]
        dma_wait(ctx, g, "p2cwe", prs)
        dma_wait(ctx, sy, "p2cbe", [(f1b[:], io["fc1bC"][:, e * 16:(e + 1) * 16])])
        sy.reg_mov(r_ld, ctx.counts["p2cld"])
        g.reg_mov(r_st, ctx.counts["p2cst"])
        g.reg_mov(r_stw, ctx.counts["p2cst"])
        issue_x1b(0)
        ctx.counts["p2cld"] += 16
        ctx.bar()
        with nc.Fori(0, NT) as i:
            load_x1b(i)
            g.wait_ge(stn, r_stw)
            ctx.bar()
            ppC.before_fill()
            ppC.after_fill(t.matmul(pC[:], oh8[:, e * 128:(e + 1) * 128],
                                    c8sb[:, ts(i, 512)], start=True, stop=True))
            ppC.before_drain()
            ppC.after_drain(v.tensor_copy(cbs[:], pC[:]))
            for m in range(16):
                bank = pA if m % 2 == 0 else pB
                ppA.before_fill()
                for k in range(NCH):
                    mm = t.matmul(bank[:], w1e[:, k * 2048 + m * 128:k * 2048 + (m + 1) * 128],
                                  xchunk(i, k), start=(k == 0), stop=(k == NCH - 1))
                ppA.after_fill(mm)
                ppA.before_drain()
                ppA.after_drain(a.activation(chunk(heb, m), bank[:], AF.Gelu,
                                             bias=f1b[:, m:m + 1]))
                ppM.after_fill(a.copy(f1b[0:1, 0:1], f1b[0:1, 0:1]))
                ppM.before_drain()
                v.tensor_mul(chunk(heb, m), chunk(heb, m), cbs[:])
            ctx.bar()
            for m in range(NCH):
                bank = pA if m % 2 == 0 else pB
                ppD.before_fill()
                for k in range(16):
                    mm = t.matmul(bank[:], w2e[:, k * 1024 + m * 128:k * 1024 + (m + 1) * 128],
                                  chunk(heb, k), start=(k == 0), stop=(k == 16 - 1))
                ppD.after_fill(mm)
                ppD.before_drain()
                ppD.after_drain(v.tensor_copy(ychunk(i, m), bank[:]))
            ctx.bar()
            g.dma_start(out=io["ydram"].rearrange("(c p) T -> p c T", p=128)[:, :, ts(i, 512)],
                        in_=ysb.ap()[:, ts(i % 2, NCH * 512)].rearrange("p (c w) -> p c w", w=512),
                        accum_op=ALU.add).then_inc(stn, 16)
            g.reg_add(r_st, r_st, 16)
            g.reg_add(r_stw, r_stw, 16)
            ctx.bar()
        g.wait_ge(stn, r_st)
        ctx.counts["p2cld"] += 16 * NT
        ctx.counts["p2cst"] += 16 * NT
        ctx.bar()

    xf = ctx.sb("p2c_xf", [128, NCH * 512], F32)
    x2c = ctx.sb("p2c_x2c", [128, NCH * 512], F32)
    x2b = hsb[:, 4096:8192]
    rb = x1b.ap()[:, 0:NCH * 512]
    sqt = hsb[:, 0:4096]
    srow = ctx.sb("p2c_srow", [1, 2048], F32)
    ctx.edges["ln_v"] = ev
    ctx.edges["ln_a"] = ctx.mk_edge(a, "p2c_ea")
    sy.reg_mov(r_ld, ctx.counts["p2cld"])
    g.reg_mov(r_st, ctx.counts["p2cst"])
    ctx.bar()
    with nc.Fori(0, NT) as i:
        sy.dma_start(out=xf.ap().rearrange("p (c w) -> p c w", w=512),
                     in_=io["x1T"].rearrange("(c p) T -> p c T", p=128)[:, :, ts(i, 512)]
                     ).then_inc(ld, 16)
        sy.dma_start(out=x2c.ap().rearrange("p (c w) -> p c w", w=512),
                     in_=io["ydram"].rearrange("(c p) T -> p c T", p=128)[:, :, ts(i, 512)]
                     ).then_inc(ld, 16)
        sy.reg_add(r_ld, r_ld, 32)
        sy.wait_ge(ld, r_ld)
        ctx.bar()
        for c in range(NCH):
            ev(v.tensor_add(chunk(xf, c), chunk(xf, c), chunk(x2c, c)))
        ctx.bar()
        emit_ln(ctx, xf, x2c, x2b, onesb, ones1, srow, pA, pB, rb, sqt)
        g.dma_start(out=io["x2T"].rearrange("(c p) T -> p c T", p=128)[:, :, ts(i, 512)],
                    in_=x2c.ap().rearrange("p (c w) -> p c w", w=512)).then_inc(stn, 16)
        g.dma_start(out=io["x2bT"][ts(i, 128)],
                    in_=x2b.rearrange("p a -> p a") if False else x2b).then_inc(stn, 16)
        g.reg_add(r_st, r_st, 32)
        g.wait_ge(stn, r_st)
        ctx.bar()
    ctx.counts["p2cld"] += 32 * NT
    ctx.counts["p2cst"] += 32 * NT
    ctx.bar()


def _mkio(nc, specs):
    io = {}
    for nm, shp, dt, kind in specs:
        h = nc.dram_tensor(nm, shp, dt, kind=kind)
        io[nm] = h.ap() if kind != "Internal" else h.ap()
    return io


def build_p2a(T):
    nc = bacc.Bacc("TRN2", target_bir_lowering=False, num_devices=8)
    NC = T // 128
    io = _mkio(nc, [
        ("x1T", [D, T], F32, "ExternalInput"),
        ("projwT", [D, 8], F32, "ExternalInput"),
        ("projb_bc", [128, 8], F32, "ExternalInput"),
        ("ebias_bc", [128, 8], F32, "ExternalInput"),
        ("identf", [128, 128], F32, "ExternalInput"),
        ("fconsts", [1, 64], F32, "ExternalInput"),
        ("W8L", [8, 2 * T], F32, "ExternalOutput"),
        ("Atm", [128, NC * 16], F32, "ExternalOutput"),
    ])
    with ExitStack() as stack:
        ctx = Ctx(nc, stack)
        ctx.setup_consts(io["fconsts"])
        emit_p2a(ctx, io, T)
    nc.finalize()
    return nc


def p2a_inputs(x1T, inp, consts):
    return {
        "x1T": np.ascontiguousarray(x1T),
        "projwT": np.ascontiguousarray(np.asarray(inp["L0_ffn_proj_w"], np.float32).T),
        "projb_bc": np.broadcast_to(np.asarray(inp["L0_ffn_proj_b"], np.float32),
                                    (128, 8)).copy(),
        "ebias_bc": np.broadcast_to(np.asarray(inp["L0_ffn_expert_bias"], np.float32),
                                    (128, 8)).copy(),
        "identf": consts["identf"],
        "fconsts": consts["fconsts"],
    }


def build_p2b(T, cap):
    nc = bacc.Bacc("TRN2", target_bir_lowering=False, num_devices=8)
    NC = T // 128
    io = _mkio(nc, [
        ("W8all", [64, 2 * T], F32, "ExternalInput"),
        ("Atm", [128, NC * 16], F32, "ExternalInput"),
        ("onesf_col", [128, 1], F32, "ExternalInput"),
        ("ones1_f", [1, 128], F32, "ExternalInput"),
        ("ltri_f", [128, 128], F32, "ExternalInput"),
        ("coremask", [128, 8], F32, "ExternalInput"),
        ("prevmask", [8, 1], F32, "ExternalInput"),
        ("identf", [128, 128], F32, "ExternalInput"),
        ("fconsts", [1, 64], F32, "ExternalInput"),
        ("coefT", [8, T], F32, "ExternalOutput"),
    ])
    with ExitStack() as stack:
        ctx = Ctx(nc, stack)
        ctx.setup_consts(io["fconsts"])
        emit_p2b(ctx, io, T, cap)
    nc.finalize()
    return nc


def p2b_inputs(W8all, Atm, consts, pcc):
    return {
        "W8all": np.ascontiguousarray(W8all),
        "Atm": np.ascontiguousarray(Atm),
        "onesf_col": np.ones((128, 1), np.float32),
        "ones1_f": consts["ones1_f"],
        "ltri_f": consts["ltri_f"],
        "coremask": pcc["coremask"],
        "prevmask": pcc["prevmask"],
        "identf": consts["identf"],
        "fconsts": consts["fconsts"],
    }


def percore_constants(core):
    prev = np.zeros((8, 1), dtype=np.float32)
    prev[:core] = 1.0
    cm = np.zeros((128, 8), dtype=np.float32)
    for cc in range(8):
        cm[16 * cc:16 * (cc + 1), cc] = 1.0
    return {"prevmask": prev, "coremask": cm}


def build_p2c(T):
    nc = bacc.Bacc("TRN2", target_bir_lowering=False, num_devices=8)
    io = _mkio(nc, [
        ("x1T", [D, T], F32, "ExternalInput"),
        ("x1bT", [T // 512 * 128, NCH * 512], BF16, "ExternalInput"),
        ("coefT", [8, T], F32, "ExternalInput"),
        ("sh1T", [D, 4096], F32, "ExternalInput"),
        ("sh2T", [4096, D], F32, "ExternalInput"),
        ("sh1bC", [128, 32], F32, "ExternalInput"),
        ("sh2bC", [128, 8], F32, "ExternalInput"),
        ("fc2b8", [8, 1024], F32, "ExternalInput"),
        ("fc1T", [8 * 1024, 2048], F32, "ExternalInput"),
        ("fc2T", [8 * 2048, 1024], F32, "ExternalInput"),
        ("fc1bC", [128, 8 * 16], F32, "ExternalInput"),
        ("oh8", [8, 8 * 128], BF16, "ExternalInput"),
        ("ones128_bf", [128, 1], BF16, "ExternalInput"),
        ("ones1_f", [1, 128], F32, "ExternalInput"),
        ("fconsts", [1, 64], F32, "ExternalInput"),
        ("ydram", [D, T], F32, "Internal"),
        ("x2T", [D, T], F32, "ExternalOutput"),
        ("x2bT", [T // 512 * 128, NCH * 512], BF16, "ExternalOutput"),
    ])
    with ExitStack() as stack:
        ctx = Ctx(nc, stack)
        ctx.setup_consts(io["fconsts"])
        emit_p2c(ctx, io, T)
    nc.finalize()
    return nc


def p2c_inputs(x1T, x1bT, coefT, inp, consts):
    oh8 = np.zeros((8, 8 * 128), BF)
    for e in range(8):
        oh8[e, e * 128:(e + 1) * 128] = 1.0
    f32 = np.float32
    fc1 = np.asarray(inp["L0_ffn_fc1_w"], f32)
    fc2 = np.asarray(inp["L0_ffn_fc2_w"], f32)
    return {
        "x1T": np.ascontiguousarray(x1T),
        "x1bT": np.ascontiguousarray(x1bT),
        "coefT": np.ascontiguousarray(coefT),
        "sh1T": np.ascontiguousarray(np.asarray(inp["L0_ffn_sh1_w"], f32).T),
        "sh2T": np.ascontiguousarray(np.asarray(inp["L0_ffn_sh2_w"], f32).T),
        "sh1bC": np.ascontiguousarray(
            np.asarray(inp["L0_ffn_sh1_b"], f32).reshape(32, 128).T),
        "sh2bC": np.ascontiguousarray(
            np.asarray(inp["L0_ffn_sh2_b"], f32).reshape(8, 128).T),
        "fc2b8": np.ascontiguousarray(np.asarray(inp["L0_ffn_fc2_b"], f32)),
        "fc1T": np.ascontiguousarray(
            fc1.transpose(0, 2, 1).reshape(8 * 1024, 2048)),
        "fc2T": np.ascontiguousarray(
            fc2.transpose(0, 2, 1).reshape(8 * 2048, 1024)),
        "fc1bC": np.ascontiguousarray(np.concatenate(
            [np.asarray(inp["L0_ffn_fc1_b"], f32)[e].reshape(16, 128).T
             for e in range(8)], axis=1)),
        "oh8": oh8,
        "ones128_bf": consts["ones128_bf"],
        "ones1_f": consts["ones1_f"],
        "fconsts": consts["fconsts"],
    }


def emit_p3(ctx, io, T):
    import os
    nc = ctx.nc
    NT = T // 512
    g, sy, t, v, a = nc.gpsimd, nc.sync, nc.tensor, nc.vector, nc.scalar
    ev = ctx.mk_edge(v, "p3_ev")
    ea = ctx.mk_edge(a, "p3_ea")
    ctx.edges["ln_v"] = ev
    ctx.edges["ln_a"] = ea

    wq = ctx.sb("p3_wq", [128, NCH * 3072], BF16)
    wgt = ctx.sb("p3_wgt", [128, NCH * 1024], BF16)
    idn = ctx.sb("p3_idn", [128, 128], BF16)
    onesb = ctx.sb("p3_onesb", [128, 1], BF16)
    ones1 = ctx.sb("p3_ones1", [1, 128], F32)
    x2b = ctx.sb("p3_x2b", [128, NCH * 512], BF16)
    qkf = ctx.sb("p3_qkf", [128, 16 * 512], BF16)
    gfm = ctx.sb("p3_gfm", [128, NCH * 512], BF16)
    s1f = ctx.sb("p3_s1f", [128, NCH * 512], F32)
    s2f = ctx.sb("p3_s2f", [128, NCH * 512], F32)
    qg = ctx.sb("p3_qg", [128, NCH * 512], BF16)
    kg = ctx.sb("p3_kg", [128, NCH * 512], BF16)
    kgtm = ctx.sb("p3_kgtm", [128, 4 * 1024], BF16)
    vtm = ctx.sb("p3_vtm", [128, 4 * 1024], BF16)
    pA = ctx.ps("p3_pA", [128, 512], F32)
    pB = ctx.ps("p3_pB", [128, 512], F32)
    pstack = ExitStack()
    pTt = pstack.enter_context(nc.psum_tensor("p3_pTt", [128, 512], BF16))
    kv0 = pstack.enter_context(nc.psum_tensor("p3_kv0", [64, 512], F32))
    kv1 = pstack.enter_context(nc.psum_tensor("p3_kv1", [64, 512], F32))
    ks0 = pstack.enter_context(nc.psum_tensor("p3_ks0", [64, 8], F32))
    ks1 = pstack.enter_context(nc.psum_tensor("p3_ks1", [64, 8], F32))

    pairs = []
    for k in range(NCH):
        pairs.append((wq[:, k * 3072:(k + 1) * 3072],
                      io["wqkv1"][128 * k:128 * (k + 1), :]))
        pairs.append((wgt[:, k * 1024:(k + 1) * 1024],
                      io["wg1"][128 * k:128 * (k + 1), :]))
    dma_wait(ctx, g, "p3w", pairs)
    dma_wait(ctx, sy, "p3c", [(idn[:], io["ident"]), (onesb[:], io["ones128_bf"]),
                              (ones1[:], io["ones1_f"])])
    ctx.bar()

    ld = ctx.sem("p3ld")
    stn = ctx.sem("p3st")
    r_ld = sy.alloc_register("r_ld3")
    r_st = g.alloc_register("r_st3")
    ppA = ctx.pipe("ppA", t, a, precharge=2)
    ppD = ctx.pipe("ppD", t, v, precharge=2)
    ppT = ctx.pipe("ppT", t, v, precharge=1)

    bgC = ctx.sb("p3_bgC", [128, NCH], F32)
    bpC = ctx.sb("p3_bpC", [128, NCH], F32)
    dma_wait(ctx, sy, "p3c2", [(bgC[:], io["bg1C"]), (bpC[:], io["bp1C"])])
    ctx.bar()

    def elu1(dst, src_qk, base_m):
        for c in range(NCH):
            ev(v.tensor_mul(chunk(s1f, c), qkf[:, (base_m + c) * 512:(base_m + c + 1) * 512],
                            chunk(gfm, c)))
        ctx.bar(BAR_DVE_ACT)
        for c in range(NCH):
            ev(v.tensor_scalar_min(chunk(s2f, c), chunk(s1f, c), 0.0))
        ctx.bar(BAR_DVE_ACT)
        for c in range(NCH):
            ea(a.activation(chunk(s2f, c), chunk(s2f, c), AF.Exp))
        ctx.bar(BAR_DVE_ACT)
        for c in range(NCH):
            ev(v.tensor_scalar_max(chunk(s1f, c), chunk(s1f, c), 0.0))
        ctx.bar(BAR_DVE_ACT)
        for c in range(NCH):
            ev(v.tensor_add(chunk(dst, c), chunk(s1f, c), chunk(s2f, c)))
        ctx.bar()

    def tile_body1(i, first):
        tcol = ts(i, 512) if not isinstance(i, int) else slice(i * 512, (i + 1) * 512)
        sy.dma_start(out=x2b[:], in_=io["x2bT"][ts(i, 128)]).then_inc(ld, 16)
        sy.reg_add(r_ld, r_ld, 16)
        sy.wait_ge(ld, r_ld)
        ctx.bar()
        for m in range(16):
            bank = pA if m % 2 == 0 else pB
            ppA.before_fill()
            for k in range(NCH):
                mm = t.matmul(bank[:], wq[:, k * 3072 + m * 128:k * 3072 + (m + 1) * 128],
                              chunk(x2b, k), start=(k == 0), stop=(k == NCH - 1))
            ppA.after_fill(mm)
            ppA.before_drain()
            ppA.after_drain(a.copy(qkf[:, m * 512:(m + 1) * 512], bank[:]))
        for m in range(NCH):
            bank = pA if m % 2 == 0 else pB
            ppA.before_fill()
            for k in range(NCH):
                mm = t.matmul(bank[:], wgt[:, k * 1024 + m * 128:k * 1024 + (m + 1) * 128],
                              chunk(x2b, k), start=(k == 0), stop=(k == NCH - 1))
            ppA.after_fill(mm)
            ppA.before_drain()
            ppA.after_drain(a.activation(chunk(gfm, m), bank[:], AF.Sigmoid,
                                         bias=bgC[:, m:m + 1]))
        ctx.bar(BAR_PE_ACT)
        for tc in range(4):
            for half in range(2):
                bank = pA if half == 0 else pB
                ppD.before_fill()
                for k in range(NCH):
                    mm = t.matmul(bank[:],
                                  x2b[:, k * 512 + tc * 128:k * 512 + (tc + 1) * 128],
                                  wq[:, k * 3072 + 2048 + half * 512:
                                        k * 3072 + 2048 + (half + 1) * 512],
                                  start=(k == 0), stop=(k == NCH - 1))
                ppD.after_fill(mm)
                ppD.before_drain()
                ppD.after_drain(v.tensor_copy(
                    vtm[:, tc * 1024 + half * 512:tc * 1024 + (half + 1) * 512],
                    bank[:]))
        ctx.bar()
        import os as _os
        if not _os.environ.get("P3NOELU"):
            elu1(qg, None, 0)
            elu1(kg, None, 8)
        g.dma_start(out=io["qgT"][ts(i, 128)], in_=qg[:]).then_inc(stn, 16)
        g.reg_add(r_st, r_st, 16)
        g.wait_ge(stn, r_st)
        for tc in range(4):
            for cg in range(2):
                ppT.before_fill()
                for cc in range(4):
                    c = cg * 4 + cc
                    mm = t.transpose(pTt[:, cc * 128:(cc + 1) * 128],
                                     kg[:, c * 512 + tc * 128:c * 512 + (tc + 1) * 128],
                                     idn[:])
                ppT.after_fill(mm)
                ppT.before_drain()
                ppT.after_drain(v.tensor_copy(
                    kgtm[:, tc * 1024 + cg * 512:tc * 1024 + (cg + 1) * 512], pTt[:]))
        ctx.bar()
        import os as _os
        for tc in ([] if _os.environ.get("P3NOKV") else range(4)):
            st_flag = False
            for j in range(8):
                for r in range(2):
                    h = 2 * j + r
                    kvp = kv0 if r == 0 else kv1
                    t.matmul(kvp[:, j * 64:(j + 1) * 64],
                             kgtm[:, tc * 1024 + h * 64:tc * 1024 + (h + 1) * 64],
                             vtm[:, tc * 1024 + h * 64:tc * 1024 + (h + 1) * 64],
                             start=st_flag, stop=False, skip_group_check=True)
            for j in range(8):
                for r in range(2):
                    h = 2 * j + r
                    ksp = ks0 if r == 0 else ks1
                    t.matmul(ksp[:, j:j + 1],
                             kgtm[:, tc * 1024 + h * 64:tc * 1024 + (h + 1) * 64],
                             onesb[:], start=st_flag, stop=False, skip_group_check=True)
        ctx.bar()

    zb0 = ctx.sb("p3_zb0", [128, 64], BF16)
    g.memset(zb0[:], 0.0)
    ctx.bar()
    if not os.environ.get("P3NOKV"):
        for j in range(8):
            for r in range(2):
                kvp = kv0 if r == 0 else kv1
                ksp = ks0 if r == 0 else ks1
                t.matmul(kvp[:, j * 64:(j + 1) * 64], zb0[:], zb0[:],
                         start=True, stop=False, skip_group_check=True)
                t.matmul(ksp[:, j:j + 1], zb0[:], zb0[:, 0:1],
                         start=True, stop=False, skip_group_check=True)
    ctx.bar()
    sy.reg_mov(r_ld, ctx.counts["p3ld"])
    g.reg_mov(r_st, ctx.counts["p3st"])
    ctx.bar()
    with nc.Fori(0, NT) as i:
        tile_body1(i, False)
    ctx.counts["p3ld"] += 16 * NT
    ctx.counts["p3st"] += 16 * NT
    ctx.bar()

    if os.environ.get("P3CUT") == "1":
        pstack.close()
        return
    zb = ctx.sb("p3_zb", [128, 64], BF16)
    g.memset(zb[:], 0.0)
    ctx.bar()
    import os as _os
    if not _os.environ.get("P3NOKV"):
        for j in range(8):
            for r in range(2):
                kvp = kv0 if r == 0 else kv1
                ksp = ks0 if r == 0 else ks1
                t.matmul(kvp[:, j * 64:(j + 1) * 64], zb[:], zb[:],
                         start=False, stop=True, skip_group_check=True)
                t.matmul(ksp[:, j:j + 1], zb[:], zb[:, 0:1],
                         start=False, stop=True, skip_group_check=True)
    ctx.bar()
    kvx = ctx.sb("p3_kvx", [128, 8 * 65], BF16)
    for r, kvp, ksp in ([] if _os.environ.get("P3NOKVX") else ((0, kv0, ks0), (1, kv1, ks1))):
        ev(v.tensor_copy(
            kvx.ap()[r * 64:(r + 1) * 64, :].rearrange("p (j c) -> p j c", c=65)[:, :, 0:64],
            kvp.ap().rearrange("p (j c) -> p j c", c=64)))
        ev(v.tensor_copy(
            kvx.ap()[r * 64:(r + 1) * 64, :].rearrange("p (j c) -> p j c", c=65)[:, :, 64:65],
            ksp.ap().rearrange("p (j c) -> p j c", c=1)))
    ctx.bar()

    pstack.close()
    if os.environ.get("P3CUT") == "2":
        return
    wp = ctx.sb("p3_wp", [128, NCH * 1024], BF16)
    xf = s1f
    ao = kg
    nb = ctx.sb("p3_nb", [64, 512], F32)
    nr = ctx.sb("p3_nr", [1, 512], F32)
    x3c = s2f
    x3b = qg
    srow = ctx.sb("p3_srow", [1, 2048], F32)
    po0 = ctx.ps("p3_po0", [128, 512], F32)
    po1 = ctx.ps("p3_po1", [128, 512], F32)
    pairs = [(wp[:, k * 1024:(k + 1) * 1024], io["wp1"][128 * k:128 * (k + 1), :])
             for k in range(NCH)]
    dma_wait(ctx, g, "p3w2", pairs)
    sy.reg_mov(r_ld, ctx.counts["p3ld"])
    g.reg_mov(r_st, ctx.counts["p3st"])
    ctx.bar()
    if os.environ.get("P3CUT") == "3":
        return

    import os as _os
    _mode = os.environ.get("P3P2MIN", "")
    with nc.Fori(0, NT) as i:
        sy.dma_start(out=x2b[:], in_=io["qgT"][ts(i, 128)]).then_inc(ld, 16)
        sy.dma_start(out=xf.ap().rearrange("p (c w) -> p c w", w=512),
                     in_=io["x2T"].rearrange("(c p) T -> p c T", p=128)[:, :, ts(i, 512)]
                     ).then_inc(ld, 16)
        sy.reg_add(r_ld, r_ld, 32)
        sy.wait_ge(ld, r_ld)
        ctx.bar()
        for j in ([] if _os.environ.get("P3NOP2") else range(8)):
            for r in range(2):
                po = po0 if r == 0 else po1
                t.matmul(po[0:65, :], kvx[r * 64:(r + 1) * 64, j * 65:(j + 1) * 65],
                         x2b[r * 64:(r + 1) * 64, j * 512:(j + 1) * 512],
                         start=True, stop=True)
            ctx.bar(BAR_PE_DVE)
            for r in range(2):
                po = po0 if r == 0 else po1
                ev(v.tensor_scalar_add(nr[:], po[64:65, :], 1.0e-8))
                ev(v.reciprocal(nr[:], nr[:]))
                ctx.bar(BAR_PE_DVE)
                t.matmul(pA[0:64, :] if r == 0 else pB[0:64, :],
                         ones1[0:1, 0:64], nr[:], start=True, stop=True)
                ctx.bar(BAR_PE_DVE)
                ev(v.tensor_copy(nb[:], (pA if r == 0 else pB)[0:64, :]))
                ctx.bar(BAR_PE_DVE)
                ev(v.tensor_mul(ao[r * 64:(r + 1) * 64, j * 512:(j + 1) * 512],
                                po[0:64, :], nb[:]))
            ctx.bar(BAR_PE_DVE)
        ctx.bar()
        for c in range(NCH):
            bank = pA if c % 2 == 0 else pB
            ppD.before_fill()
            for k in range(NCH):
                mm = t.matmul(bank[:], wp[:, k * 1024 + c * 128:k * 1024 + (c + 1) * 128],
                              chunk(ao, k), start=(k == 0), stop=(k == NCH - 1))
            ppD.after_fill(mm)
            ppD.before_drain()
            op1 = v.tensor_scalar_add(chunk(x3c, c), bank[:], bpC[:, c:c + 1])
            ppD.after_drain(op1)
            ev(v.tensor_copy(nr[:, 0:1], nr[:, 0:1]))
            ev(v.tensor_add(chunk(xf, c), chunk(xf, c), chunk(x3c, c)))
        ctx.bar()
        if os.environ.get("P3NOLN"):
            for c in range(NCH):
                ev(v.tensor_copy(chunk(x3c, c), chunk(xf, c)))
                ev(v.tensor_copy(chunk(x3b, c), chunk(xf, c)))
            ctx.bar()
        else:
            emit_ln(ctx, xf, x3c, x3b, onesb, ones1, srow, pA, pB, x2b, gfm)
        g.dma_start(out=io["x3T"].rearrange("(c p) T -> p c T", p=128)[:, :, ts(i, 512)],
                    in_=x3c.ap().rearrange("p (c w) -> p c w", w=512)).then_inc(stn, 16)
        g.dma_start(out=io["x3bT"][ts(i, 128)], in_=x3b[:]).then_inc(stn, 16)
        g.reg_add(r_st, r_st, 32)
        g.wait_ge(stn, r_st)
        ctx.bar()
    ctx.counts["p3ld"] += 32 * NT
    ctx.counts["p3st"] += 32 * NT
    ctx.bar()


def build_p3(T):
    nc = bacc.Bacc("TRN2", target_bir_lowering=False, num_devices=8)
    io = _mkio(nc, [
        ("x2T", [D, T], F32, "ExternalInput"),
        ("x2bT", [T // 512 * 128, NCH * 512], BF16, "ExternalInput"),
        ("wqkv1", [D, 3072], F32, "ExternalInput"),
        ("wg1", [D, D], F32, "ExternalInput"),
        ("wp1", [D, D], F32, "ExternalInput"),
        ("bg1C", [128, NCH], F32, "ExternalInput"),
        ("bp1C", [128, NCH], F32, "ExternalInput"),
        ("bg_bc", [128, 16], F32, "ExternalInput"),
        ("ident", [128, 128], BF16, "ExternalInput"),
        ("ones128_bf", [128, 1], BF16, "ExternalInput"),
        ("ones1_f", [1, 128], F32, "ExternalInput"),
        ("fconsts", [1, 64], F32, "ExternalInput"),
        ("qgT", [T // 512 * 128, NCH * 512], BF16, "Internal"),
        ("x3T", [D, T], F32, "ExternalOutput"),
        ("x3bT", [T // 512 * 128, NCH * 512], BF16, "ExternalOutput"),
    ])
    with ExitStack() as stack:
        ctx = Ctx(nc, stack)
        ctx.setup_consts(io["fconsts"])
        emit_p3(ctx, io, T)
    nc.finalize()
    return nc


def p3_inputs(x2T, x2bT, inp, consts):
    f32 = np.float32
    return {
        "x2T": np.ascontiguousarray(x2T),
        "x2bT": np.ascontiguousarray(x2bT),
        "wqkv1": np.ascontiguousarray(np.asarray(inp["L1_attn_Wqkv"], f32)),
        "wg1": np.ascontiguousarray(np.asarray(inp["L1_attn_Wg"], f32)),
        "wp1": np.ascontiguousarray(np.asarray(inp["L1_attn_Wp"], f32)),
        "bg1C": np.ascontiguousarray(np.asarray(inp["L1_attn_bg"], f32).reshape(NCH, 128).T),
        "bp1C": np.ascontiguousarray(np.asarray(inp["L1_attn_bp"], f32).reshape(NCH, 128).T),
        "bg_bc": np.zeros((128, 16), f32),
        "ident": consts["ident"],
        "ones128_bf": consts["ones128_bf"],
        "ones1_f": consts["ones1_f"],
        "fconsts": consts["fconsts"],
    }


def emit_p4(ctx, io, T):
    nc = ctx.nc
    NT = T // 512
    g, sy, t, v, a = nc.gpsimd, nc.sync, nc.tensor, nc.vector, nc.scalar
    ev = ctx.mk_edge(v, "p4_ev")
    ea = ctx.mk_edge(a, "p4_ea")
    ctx.edges["ln_v"] = ev
    ctx.edges["ln_a"] = ea

    onesb = ctx.sb("p4_onesb", [128, 1], BF16)
    ones1 = ctx.sb("p4_ones1", [1, 128], F32)
    idnf = ctx.sb("p4_idnf", [128, 128], F32)
    oh16 = ctx.sb("p4_oh16", [16, 16 * 128], BF16)
    gw = ctx.sb("p4_gw", [128, NCH * 16], F32)
    gbb = ctx.sb("p4_gbb", [128, 16], F32)
    be2s = ctx.sb("p4_be2s", [16, 1024], BF16)
    bs1C = ctx.sb("p4_bs1C", [128, NCH], F32)
    bs3C = ctx.sb("p4_bs3C", [128, NCH], F32)
    bs2C = ctx.sb("p4_bs2C", [128, NCH], F32)
    be1C = ctx.sb("p4_be1C", [128, 16 * 4], F32)
    be3C = ctx.sb("p4_be3C", [128, 16 * 4], F32)
    c16 = ctx.sb("p4_c16", [16, T], BF16)
    x3f = ctx.sb("p4_x3f", [128, NCH * 512], F32)
    x3b = ctx.sb("p4_x3b", [128, 2 * NCH * 512], BF16)
    wb1 = ctx.sb("p4_wb1", [128, 16384], BF16)
    wb2 = ctx.sb("p4_wb2", [128, 16384], BF16)
    wb3 = ctx.sb("p4_wb3", [128, 16384], BF16)
    hbuf = ctx.sb("p4_hbuf", [128, 16 * 512], BF16)
    ysb = ctx.sb("p4_ysb", [128, 2 * NCH * 512], F32)

    def y4chunk(i, m):
        par = i % 2
        return ysb.ap()[:, ts(par, NCH * 512)][:, m * 512:(m + 1) * 512]

    r_stw = g.alloc_register("r_stw4")
    cbs = ctx.sb("p4_cbs", [128, 512], BF16)
    t16a = ctx.sb("p4_t16a", [128, 16], F32)
    t16b = ctx.sb("p4_t16b", [128, 16], F32)
    t16c = ctx.sb("p4_t16c", [128, 16], F32)
    t16d = ctx.sb("p4_t16d", [128, 16], F32)
    t4a = ctx.sb("p4_t4a", [128, 4], F32)
    t4b = ctx.sb("p4_t4b", [128, 4], F32)
    t4c = ctx.sb("p4_t4c", [128, 4], F32)
    c1a = ctx.sb("p4_c1a", [128, 1], F32)
    c1b = ctx.sb("p4_c1b", [128, 1], F32)
    c1c = ctx.sb("p4_c1c", [128, 1], F32)
    c1d = ctx.sb("p4_c1d", [128, 1], F32)
    ones4 = ctx.sb("p4_ones4", [128, 4], F32)
    pA = ctx.ps("p4_pA", [128, 512], F32)
    pB = ctx.ps("p4_pB", [128, 512], F32)
    pC = ctx.ps("p4_pC", [128, 512], F32)
    pTr = ctx.ps("p4_pTr", [16, 128], F32)

    dma_wait(ctx, sy, "p4c", [
        (onesb[:], io["ones128_bf"]), (ones1[:], io["ones1_f"]),
        (idnf[:], io["identf"]), (oh16[:], io["oh16"]),
        (gbb[:], io["gateb_bc"]), (be2s[:], io["be2s"]),
        (bs1C[:], io["bs1C"]), (bs3C[:], io["bs3C"]), (bs2C[:], io["bs2C"]),
        (be1C[:], io["be1C"]), (be3C[:], io["be3C"]),
    ])
    prs = [(gw[:, k * 16:(k + 1) * 16], io["gatewT"][128 * k:128 * (k + 1), :])
           for k in range(NCH)]
    dma_wait(ctx, g, "p4gw", prs)
    g.memset(ones4[:], 1.0)
    ctx.bar()

    ld = ctx.sem("p4ld")
    stn = ctx.sem("p4st")
    r_ld = sy.alloc_register("r_ld4")
    r_st = g.alloc_register("r_st4")
    ppA = ctx.pipe("ppA", t, a, precharge=2)
    ppD = ctx.pipe("ppD", t, v, precharge=2)
    ppC = ctx.pipe("ppC", t, v, precharge=1)

    sy.reg_mov(r_ld, ctx.counts["p4ld"])
    ctx.bar()
    with nc.Fori(0, NT) as i:
        sy.dma_start(out=x3f.ap().rearrange("p (c w) -> p c w", w=512),
                     in_=io["x3T"].rearrange("(c p) T -> p c T", p=128)[:, :, ts(i, 512)]
                     ).then_inc(ld, 16)
        sy.reg_add(r_ld, r_ld, 16)
        sy.wait_ge(ld, r_ld)
        ctx.bar()
        for tc4 in range(4):
            for k in range(NCH):
                t.matmul(pA[0:128, 0:16],
                         x3f[:, k * 512 + tc4 * 128:k * 512 + (tc4 + 1) * 128],
                         gw[:, k * 16:(k + 1) * 16], start=(k == 0), stop=(k == NCH - 1))
            ctx.bar(BAR_PE_DVE_ACT)
            ev(v.reduce_max(c1a[:], pA[0:128, 0:16], axis=AX.X, negate=True))
            ctx.bar(BAR_PE_DVE_ACT)
            ea(a.activation(t16a[:], pA[0:128, 0:16], AF.Exp, bias=c1a[:],
                            accum_out=c1b[:]))
            ctx.bar(BAR_DVE_ACT)
            ev(v.reciprocal(c1c[:], c1b[:]))
            ev(v.tensor_scalar(t16a[:], t16a[:], c1c[:], None, op0=ALU.mult))
            ev(v.tensor_add(t16b[:], t16a[:], gbb[:]))
            for gg in range(4):
                sl = t16b[:, gg * 4:(gg + 1) * 4]
                ev(v.reduce_max(c1a[:], sl, axis=AX.X))
                ev(v.tensor_scalar(t4a[:], sl, c1a[:], None, op0=ALU.is_equal,
                                   op1=ALU.add, accum_out=c1b[:]))
                ev(v.scalar_tensor_tensor(t4b[:], t4a[:], -1.0e30, sl,
                                          op0=ALU.mult, op1=ALU.add))
                ev(v.reduce_max(c1c[:], t4b[:], axis=AX.X))
                ev(v.tensor_scalar(c1d[:], c1b[:], 2.0, None, op0=ALU.is_ge))
                ev(v.tensor_sub(c1b[:], c1a[:], c1c[:]))
                ev(v.tensor_mul(c1b[:], c1b[:], c1d[:]))
                ev(v.tensor_add(c1b[:], c1b[:], c1c[:]))
                ev(v.tensor_add(t4c[:, gg:gg + 1], c1a[:], c1b[:]))
            ev(v.reduce_max(c1a[:], t4c[:], axis=AX.X))
            ev(v.tensor_scalar(t4b[:], t4c[:], c1a[:], None, op0=ALU.is_equal))
            ev(v.tensor_tensor_scan(t4a[:], t4b[:], t4b[:], 0.0,
                                    op0=ALU.add, op1=ALU.bypass))
            ev(v.tensor_scalar(t4a[:], t4a[:], 1.0, None, op0=ALU.is_equal))
            ev(v.tensor_mul(t4b[:], t4b[:], t4a[:]))
            ev(v.scalar_tensor_tensor(t4a[:], t4b[:], -1.0e30, t4c[:],
                                      op0=ALU.mult, op1=ALU.add))
            ev(v.reduce_max(c1b[:], t4a[:], axis=AX.X))
            ev(v.tensor_scalar(t4c[:], t4a[:], c1b[:], None, op0=ALU.is_equal))
            ev(v.tensor_tensor_scan(t4a[:], t4c[:], t4c[:], 0.0,
                                    op0=ALU.add, op1=ALU.bypass))
            ev(v.tensor_scalar(t4a[:], t4a[:], 1.0, None, op0=ALU.is_equal))
            ev(v.tensor_mul(t4c[:], t4c[:], t4a[:]))
            ev(v.tensor_add(t4a[:], t4b[:], t4c[:]))
            for gg in range(4):
                ev(v.tensor_scalar(t16c[:, gg * 4:(gg + 1) * 4], ones4[:],
                                   t4a[:, gg:gg + 1], None, op0=ALU.mult))
            ev(v.tensor_scalar(t16d[:], t16c[:], 1.0e9, -1.0e9,
                               op0=ALU.mult, op1=ALU.add))
            ev(v.tensor_add(t16b[:], t16b[:], t16d[:]))
            ev(v.reduce_max(c1a[:], t16b[:], axis=AX.X))
            ev(v.tensor_scalar(t16c[:], t16b[:], c1a[:], None, op0=ALU.is_equal))
            ev(v.tensor_tensor_scan(t16d[:], t16c[:], t16c[:], 0.0,
                                    op0=ALU.add, op1=ALU.bypass))
            ev(v.tensor_scalar(t16d[:], t16d[:], 1.0, None, op0=ALU.is_equal))
            ev(v.tensor_mul(t16c[:], t16c[:], t16d[:]))
            ev(v.scalar_tensor_tensor(t16d[:], t16c[:], -1.0e30, t16b[:],
                                      op0=ALU.mult, op1=ALU.add))
            ev(v.reduce_max(c1b[:], t16d[:], axis=AX.X))
            ev(v.tensor_scalar(t16d[:], t16d[:], c1b[:], None, op0=ALU.is_equal))
            ev(v.tensor_tensor_scan(t16b[:], t16d[:], t16d[:], 0.0,
                                    op0=ALU.add, op1=ALU.bypass))
            ev(v.tensor_scalar(t16b[:], t16b[:], 1.0, None, op0=ALU.is_equal))
            ev(v.tensor_mul(t16d[:], t16d[:], t16b[:]))
            ev(v.scalar_tensor_tensor(t16b[:], t16c[:], 1.0, t16a[:],
                                      op0=ALU.mult, op1=ALU.mult, accum_out=c1a[:]))
            ev(v.scalar_tensor_tensor(t16b[:], t16d[:], 1.0, t16a[:],
                                      op0=ALU.mult, op1=ALU.mult, accum_out=c1b[:]))
            ev(v.tensor_scalar(t16c[:], t16c[:], c1a[:], None, op0=ALU.mult))
            ev(v.tensor_scalar(t16d[:], t16d[:], c1b[:], None, op0=ALU.mult))
            ev(v.tensor_add(t16c[:], t16c[:], t16d[:]))
            ctx.bar()
            t.transpose(pTr[:], t16c[:], idnf[:])
            ctx.bar(BAR_PE_DVE)
            ev(v.tensor_copy(c16.ap()[:, bass.ts(i, 512)][:, tc4 * 128:(tc4 + 1) * 128],
                             pTr[:]))
            ctx.bar(BAR_PE_DVE)
        ctx.bar()
    ctx.counts["p4ld"] += 16 * NT
    ctx.bar()

    prs = []
    for k in range(NCH):
        prs.append((wb1[:, k * 1024:(k + 1) * 1024],
                    io["ws1T"][128 * k:128 * (k + 1), :]))
        prs.append((wb2[:, k * 1024:(k + 1) * 1024],
                    io["ws3T"][128 * k:128 * (k + 1), :]))
        prs.append((wb3[:, k * 1024:(k + 1) * 1024],
                    io["ws2T"][128 * k:128 * (k + 1), :]))
    dma_wait(ctx, g, "p4ws", prs)
    h1s = hbuf[:, 0:8 * 512]
    hh3 = hbuf[:, 8 * 512:16 * 512]

    def issue_x3b(i):
        par = i % 2
        sy.dma_start(out=x3b.ap()[:, ts(par, NCH * 512)],
                     in_=io["x3bT"][ts(i, 128)]).then_inc(ld, 16)

    def load_x3b(i):
        sy.reg_add(r_ld, r_ld, 16)
        sy.wait_ge(ld, r_ld)
        issue_x3b((i + 1) % NT)

    def x3chunk(i, c):
        par = i % 2
        return x3b.ap()[:, ts(par, NCH * 512)][:, c * 512:(c + 1) * 512]

    sy.reg_mov(r_ld, ctx.counts["p4ld"])
    g.reg_mov(r_st, ctx.counts["p4st"])
    g.reg_mov(r_stw, ctx.counts["p4st"])
    issue_x3b(0)
    ctx.counts["p4ld"] += 16
    ctx.bar()
    with nc.Fori(0, NT) as i:
        load_x3b(i)
        g.wait_ge(stn, r_stw)
        ctx.bar()
        for m in range(NCH):
            bank = pA if m % 2 == 0 else pB
            ppA.before_fill()
            for k in range(NCH):
                mm = t.matmul(bank[:], wb1[:, k * 1024 + m * 128:k * 1024 + (m + 1) * 128],
                              x3chunk(i, k), start=(k == 0), stop=(k == NCH - 1))
            ppA.after_fill(mm)
            ppA.before_drain()
            ppA.after_drain(a.activation(chunk(h1s, m), bank[:], AF.Silu,
                                         bias=bs1C[:, m:m + 1]))
        ctx.bar()
        for m in range(NCH):
            bank = pA if m % 2 == 0 else pB
            ppD.before_fill()
            for k in range(NCH):
                mm = t.matmul(bank[:], wb2[:, k * 1024 + m * 128:k * 1024 + (m + 1) * 128],
                              x3chunk(i, k), start=(k == 0), stop=(k == NCH - 1))
            ppD.after_fill(mm)
            ppD.before_drain()
            ppD.after_drain(v.scalar_tensor_tensor(chunk(hh3, m), bank[:],
                                                   bs3C[:, m:m + 1], chunk(h1s, m),
                                                   op0=ALU.add, op1=ALU.mult))
        ctx.bar()
        for m in range(NCH):
            bank = pA if m % 2 == 0 else pB
            ppD.before_fill()
            for k in range(NCH):
                mm = t.matmul(bank[:], wb3[:, k * 1024 + m * 128:k * 1024 + (m + 1) * 128],
                              chunk(hh3, k), start=(k == 0), stop=(k == NCH - 1))
            ppD.after_fill(mm)
            ppC.before_fill()
            ppC.after_fill(t.matmul(pC[:], be2s[:, m * 128:(m + 1) * 128],
                                    c16[:, ts(i, 512)], start=True, stop=True))
            ppD.before_drain()
            op1 = v.tensor_scalar_add(y4chunk(i, m), bank[:], bs2C[:, m:m + 1])
            ppD.after_drain(op1)
            ev(v.tensor_copy(cbs[:, 0:1], cbs[:, 0:1]))
            ppC.before_drain()
            ppC.after_drain(v.tensor_add(y4chunk(i, m), y4chunk(i, m), pC[:]))
        ctx.bar()
        g.dma_start(out=io["ydram1"].rearrange("(c p) T -> p c T", p=128)[:, :, ts(i, 512)],
                    in_=ysb.ap()[:, ts(i % 2, NCH * 512)].rearrange("p (c w) -> p c w", w=512)
                    ).then_inc(stn, 16)
        g.reg_add(r_st, r_st, 16)
        g.reg_add(r_stw, r_stw, 16)
        ctx.bar()
    g.wait_ge(stn, r_st)
    ctx.counts["p4ld"] += 16 * NT
    ctx.counts["p4st"] += 16 * NT
    ctx.bar()

    for grp in range(4):
        prs = []
        for ei in range(4):
            e = grp * 4 + ei
            for k in range(NCH):
                prs.append((wb1[:, ei * 4096 + k * 512:ei * 4096 + (k + 1) * 512],
                            io["we1T"][e * 1024 + 128 * k:e * 1024 + 128 * (k + 1), :]))
                prs.append((wb2[:, ei * 4096 + k * 512:ei * 4096 + (k + 1) * 512],
                            io["we3T"][e * 1024 + 128 * k:e * 1024 + 128 * (k + 1), :]))
            for k in range(4):
                prs.append((wb3[:, ei * 4096 + k * 1024:ei * 4096 + (k + 1) * 1024],
                            io["we2T"][e * 512 + 128 * k:e * 512 + 128 * (k + 1), :]))
        dma_wait(ctx, g, "p4we", prs)
        sy.reg_mov(r_ld, ctx.counts["p4ld"])
        g.reg_mov(r_st, ctx.counts["p4st"])
        g.reg_mov(r_stw, ctx.counts["p4st"])
        issue_x3b(0)
        ctx.counts["p4ld"] += 16
        ctx.bar()
        with nc.Fori(0, NT) as i:
            load_x3b(i)
            g.wait_ge(stn, r_stw)
            ctx.bar()
            for ei in range(4):
                e = grp * 4 + ei
                for m in range(4):
                    bank = pA if m % 2 == 0 else pB
                    ppA.before_fill()
                    for k in range(NCH):
                        mm = t.matmul(bank[:],
                                      wb1[:, ei * 4096 + k * 512 + m * 128:
                                            ei * 4096 + k * 512 + (m + 1) * 128],
                                      x3chunk(i, k), start=(k == 0), stop=(k == NCH - 1))
                    ppA.after_fill(mm)
                    ppA.before_drain()
                    ppA.after_drain(a.activation(h1s[:, m * 512:(m + 1) * 512], bank[:],
                                                 AF.Silu,
                                                 bias=be1C[:, e * 4 + m:e * 4 + m + 1]))
                ctx.bar()
                ppC.before_fill()
                ppC.after_fill(t.matmul(pC[:], oh16[:, e * 128:(e + 1) * 128],
                                        c16[:, ts(i, 512)], start=True, stop=True))
                ppC.before_drain()
                ppC.after_drain(v.tensor_copy(cbs[:], pC[:]))
                for m in range(4):
                    bank = pA if m % 2 == 0 else pB
                    ppD.before_fill()
                    for k in range(NCH):
                        mm = t.matmul(bank[:],
                                      wb2[:, ei * 4096 + k * 512 + m * 128:
                                            ei * 4096 + k * 512 + (m + 1) * 128],
                                      x3chunk(i, k), start=(k == 0), stop=(k == NCH - 1))
                    ppD.after_fill(mm)
                    ppD.before_drain()
                    op1 = v.scalar_tensor_tensor(hh3[:, m * 512:(m + 1) * 512], bank[:],
                                                 be3C[:, e * 4 + m:e * 4 + m + 1],
                                                 h1s[:, m * 512:(m + 1) * 512],
                                                 op0=ALU.add, op1=ALU.mult)
                    ppD.after_drain(op1)
                    ev(v.tensor_copy(cbs[:, 0:1], cbs[:, 0:1]))
                    ev(v.tensor_mul(hh3[:, m * 512:(m + 1) * 512],
                                    hh3[:, m * 512:(m + 1) * 512], cbs[:]))
                ctx.bar(BAR_PE_DVE)
                for m in range(NCH):
                    bank = pA if m % 2 == 0 else pB
                    ppD.before_fill()
                    for k in range(4):
                        mm = t.matmul(bank[:],
                                      wb3[:, ei * 4096 + k * 1024 + m * 128:
                                            ei * 4096 + k * 1024 + (m + 1) * 128],
                                      hh3[:, k * 512:(k + 1) * 512],
                                      start=(k == 0), stop=(k == 3))
                    ppD.after_fill(mm)
                    ppD.before_drain()
                    ppD.after_drain(v.tensor_copy(y4chunk(i, m), bank[:]) if ei == 0 else
                                    v.tensor_add(y4chunk(i, m), y4chunk(i, m), bank[:]))
                ctx.bar(BAR_PE_DVE)
            ctx.bar()
            g.dma_start(out=io["ydram1"].rearrange("(c p) T -> p c T", p=128)[:, :, ts(i, 512)],
                        in_=ysb.ap()[:, ts(i % 2, NCH * 512)].rearrange("p (c w) -> p c w", w=512),
                        accum_op=ALU.add).then_inc(stn, 16)
            g.reg_add(r_st, r_st, 16)
            g.reg_add(r_stw, r_stw, 16)
            ctx.bar()
        g.wait_ge(stn, r_st)
        ctx.counts["p4ld"] += 16 * NT
        ctx.counts["p4st"] += 16 * NT
        ctx.bar()

    srow = ysb.ap()[0:1, 0:2048]
    rb = x3b.ap()[:, 0:NCH * 512]
    sqt = hbuf[:, 0:4096]
    x4b = hbuf[:, 4096:8192]
    r_ld2 = g.alloc_register("r_ld4b")
    sy.reg_mov(r_ld, ctx.counts["p4ld"])
    g.reg_mov(r_ld2, ctx.counts["p4ld"])
    g.reg_mov(r_st, ctx.counts["p4st"])
    ctx.bar()
    with nc.Fori(0, NT) as i:
        sy.dma_start(out=x3f.ap().rearrange("p (c w) -> p c w", w=512),
                     in_=io["x3T"].rearrange("(c p) T -> p c T", p=128)[:, :, ts(i, 512)]
                     ).then_inc(ld, 16)
        sy.reg_add(r_ld, r_ld, 16)
        sy.wait_ge(ld, r_ld)
        ctx.bar()
        g.dma_start(out=x3f.ap().rearrange("p (c w) -> p c w", w=512),
                    in_=io["ydram1"].rearrange("(c p) T -> p c T", p=128)[:, :, ts(i, 512)],
                    accum_op=ALU.add).then_inc(ld, 16)
        g.reg_add(r_ld2, r_ld2, 32)
        g.wait_ge(ld, r_ld2)
        sy.reg_add(r_ld, r_ld, 16)
        ctx.bar()
        emit_ln(ctx, x3f, x3f, x4b, onesb, ones1, srow, pA, pB, rb, sqt)
        g.dma_start(out=io["x4T"].rearrange("(c p) T -> p c T", p=128)[:, :, ts(i, 512)],
                    in_=x3f.ap().rearrange("p (c w) -> p c w", w=512)).then_inc(stn, 16)
        g.reg_add(r_st, r_st, 16)
        g.wait_ge(stn, r_st)
        ctx.bar()
    ctx.counts["p4ld"] += 32 * NT
    ctx.counts["p4st"] += 16 * NT
    ctx.bar()


def build_p4(T):
    nc = bacc.Bacc("TRN2", target_bir_lowering=False, num_devices=8)
    io = _mkio(nc, [
        ("x3T", [D, T], F32, "ExternalInput"),
        ("x3bT", [T // 512 * 128, NCH * 512], BF16, "ExternalInput"),
        ("gatewT", [D, 16], F32, "ExternalInput"),
        ("gateb_bc", [128, 16], F32, "ExternalInput"),
        ("ws1T", [D, D], F32, "ExternalInput"),
        ("ws3T", [D, D], F32, "ExternalInput"),
        ("ws2T", [D, D], F32, "ExternalInput"),
        ("bs1C", [128, NCH], F32, "ExternalInput"),
        ("bs3C", [128, NCH], F32, "ExternalInput"),
        ("bs2C", [128, NCH], F32, "ExternalInput"),
        ("we1T", [16 * 1024, 512], F32, "ExternalInput"),
        ("we3T", [16 * 1024, 512], F32, "ExternalInput"),
        ("we2T", [16 * 512, 1024], F32, "ExternalInput"),
        ("be1C", [128, 64], F32, "ExternalInput"),
        ("be3C", [128, 64], F32, "ExternalInput"),
        ("be2s", [16, 1024], BF16, "ExternalInput"),
        ("oh16", [16, 16 * 128], BF16, "ExternalInput"),
        ("identf", [128, 128], F32, "ExternalInput"),
        ("ones128_bf", [128, 1], BF16, "ExternalInput"),
        ("ones1_f", [1, 128], F32, "ExternalInput"),
        ("fconsts", [1, 64], F32, "ExternalInput"),
        ("ydram1", [D, T], F32, "Internal"),
        ("x4T", [D, T], F32, "ExternalOutput"),
    ])
    with ExitStack() as stack:
        ctx = Ctx(nc, stack)
        ctx.setup_consts(io["fconsts"])
        emit_p4(ctx, io, T)
    nc.finalize()
    return nc


def p4_inputs(x3T, x3bT, inp, consts):
    f32 = np.float32
    oh16 = np.zeros((16, 16 * 128), BF)
    for e in range(16):
        oh16[e, e * 128:(e + 1) * 128] = 1.0
    be1 = np.asarray(inp["L1_ffn_be1"], f32)
    be3 = np.asarray(inp["L1_ffn_be3"], f32)
    return {
        "x3T": np.ascontiguousarray(x3T),
        "x3bT": np.ascontiguousarray(x3bT),
        "gatewT": np.ascontiguousarray(np.asarray(inp["L1_ffn_gate_w"], f32).T),
        "gateb_bc": np.broadcast_to(np.asarray(inp["L1_ffn_gate_b"], f32),
                                    (128, 16)).copy(),
        "ws1T": np.ascontiguousarray(np.asarray(inp["L1_ffn_ws1"], f32)),
        "ws3T": np.ascontiguousarray(np.asarray(inp["L1_ffn_ws3"], f32)),
        "ws2T": np.ascontiguousarray(np.asarray(inp["L1_ffn_ws2"], f32)),
        "bs1C": np.ascontiguousarray(np.asarray(inp["L1_ffn_bs1"], f32).reshape(NCH, 128).T),
        "bs3C": np.ascontiguousarray(np.asarray(inp["L1_ffn_bs3"], f32).reshape(NCH, 128).T),
        "bs2C": np.ascontiguousarray(np.asarray(inp["L1_ffn_bs2"], f32).reshape(NCH, 128).T),
        "we1T": np.ascontiguousarray(np.asarray(inp["L1_ffn_We1"], f32).reshape(16 * 1024, 512)),
        "we3T": np.ascontiguousarray(np.asarray(inp["L1_ffn_We3"], f32).reshape(16 * 1024, 512)),
        "we2T": np.ascontiguousarray(np.asarray(inp["L1_ffn_We2"], f32).reshape(16 * 512, 1024)),
        "be1C": np.ascontiguousarray(np.concatenate(
            [be1[e].reshape(4, 128).T for e in range(16)], axis=1)),
        "be3C": np.ascontiguousarray(np.concatenate(
            [be3[e].reshape(4, 128).T for e in range(16)], axis=1)),
        "be2s": np.ascontiguousarray(np.asarray(inp["L1_ffn_be2"], f32)).astype(BF),
        "oh16": oh16,
        "identf": consts["identf"],
        "ones128_bf": consts["ones128_bf"],
        "ones1_f": consts["ones1_f"],
        "fconsts": consts["fconsts"],
    }


def build_fused(T, cap):
    nc = bacc.Bacc("TRN2", target_bir_lowering=False, num_devices=8)
    NC = T // 128
    ext = [
        ("xT", [D, T], F32),
        ("wqkv", [D, 3072], F32), ("wgates", [D, 16], F32), ("wout", [D, D], F32),
        ("gammaC", [128, NCH], F32), ("bgates_bc", [128, 16], F32),
        ("ident", [128, 128], BF16), ("identf", [128, 128], F32),
        ("ones128_bf", [128, 1], BF16), ("ones1_f", [1, 128], F32),
        ("ones1_bf", [1, 128], BF16), ("bgates_row_bf", [1, 16], BF16),
        ("onesf_col", [128, 1], F32), ("ltri_f", [128, 128], F32),
        ("mask0", [128, 512], F32), ("mask1", [128, 512], F32),
        ("maskf0", [128, 512], F32), ("maskf1", [128, 512], F32),
        ("fconsts", [1, 64], F32),
        ("projwT", [D, 8], F32), ("projb_bc", [128, 8], F32), ("ebias_bc", [128, 8], F32),
        ("coremask", [128, 8], F32), ("prevmask", [8, 1], F32),
        ("sh1T", [D, 4096], F32), ("sh2T", [4096, D], F32),
        ("sh1bC", [128, 32], F32), ("sh2bC", [128, 8], F32), ("fc2b8", [8, 1024], F32),
        ("fc1T", [8 * 1024, 2048], F32), ("fc2T", [8 * 2048, 1024], F32),
        ("fc1bC", [128, 8 * 16], F32), ("oh8", [8, 8 * 128], BF16),
        ("wqkv1", [D, 3072], F32), ("wg1", [D, D], F32), ("wp1", [D, D], F32),
        ("bg1C", [128, NCH], F32), ("bp1C", [128, NCH], F32),
        ("gatewT", [D, 16], F32), ("gateb_bc", [128, 16], F32),
        ("ws1T", [D, D], F32), ("ws3T", [D, D], F32), ("ws2T", [D, D], F32),
        ("bs1C", [128, NCH], F32), ("bs3C", [128, NCH], F32), ("bs2C", [128, NCH], F32),
        ("we1T", [16 * 1024, 512], F32), ("we3T", [16 * 1024, 512], F32),
        ("we2T", [16 * 512, 1024], F32),
        ("be1C", [128, 64], F32), ("be3C", [128, 64], F32), ("be2s", [16, 1024], BF16),
        ("oh16", [16, 16 * 128], BF16),
    ]
    internal = [
        ("x1T", [D, T], F32), ("x1bT", [T // 512 * 128, NCH * 512], BF16),
        ("W8L", [8, 2 * T], F32), ("Atm", [128, NC * 16], F32),
        ("W8all", [64, 2 * T], F32), ("coefT", [8, T], F32),
        ("ydram", [D, T], F32), ("x2T", [D, T], F32), ("x2bT", [T // 512 * 128, NCH * 512], BF16),
        ("qgT", [T // 512 * 128, NCH * 512], BF16), ("x3T", [D, T], F32), ("x3bT", [T // 512 * 128, NCH * 512], BF16),
        ("ydram1", [D, T], F32),
    ]
    io = {}
    for nm, shp, dt in ext:
        io[nm] = nc.dram_tensor(nm, shp, dt, kind="ExternalInput").ap()
    for nm, shp, dt in internal:
        io[nm] = nc.dram_tensor(nm, shp, dt, kind="Internal").ap()
    io["x4T"] = nc.dram_tensor("x4T", [D, T], F32, kind="ExternalOutput").ap()

    with ExitStack() as stack:
        ctx = Ctx(nc, stack)
        ctx.setup_consts(io["fconsts"])
        g = nc.gpsimd
        ctx.push(); emit_p1(ctx, io, T); ctx.pop()
        nc.all_engine_barrier()
        ctx.push(); emit_p2a(ctx, io, T); ctx.pop()
        nc.all_engine_barrier()
        cc = ctx.sem("ccag")
        g.collective_compute("AllGather", ALU.bypass,
                             replica_groups=[list(range(8))],
                             ins=[io["W8L"].opt()],
                             outs=[io["W8all"].opt()]).then_inc(cc, 1)
        g.wait_ge(cc, 1)
        nc.all_engine_barrier()
        ctx.push(); emit_p2b(ctx, io, T, cap); ctx.pop()
        nc.all_engine_barrier()
        ctx.push(); emit_p2c(ctx, io, T); ctx.pop()
        nc.all_engine_barrier()
        ctx.push(); emit_p3(ctx, io, T); ctx.pop()
        nc.all_engine_barrier()
        ctx.push(); emit_p4(ctx, io, T); ctx.pop()
        nc.all_engine_barrier()
    nc.finalize()
    return nc


def fused_inputs(core, x, inputs, consts):
    m = {}
    m.update(p1_inputs(x[core], inputs, consts))
    m.pop("xT")
    m["xT"] = np.ascontiguousarray(x[core].T)
    a = p2a_inputs(np.zeros((1, 1), np.float32), inputs, consts)
    a.pop("x1T")
    m.update(a)
    pcc = percore_constants(core)
    m["coremask"] = pcc["coremask"]
    m["prevmask"] = pcc["prevmask"]
    m["onesf_col"] = np.ones((128, 1), np.float32)
    m["ltri_f"] = consts["ltri_f"]
    c = p2c_inputs(np.zeros((1, 1), np.float32), np.zeros((1, 1), BF),
                   np.zeros((1, 1), np.float32), inputs, consts)
    for k in ("x1T", "x1bT", "coefT"):
        c.pop(k)
    m.update(c)
    p3 = p3_inputs(np.zeros((1, 1), np.float32), np.zeros((1, 1), BF), inputs, consts)
    for k in ("x2T", "x2bT"):
        p3.pop(k)
    m.update(p3)
    p4 = p4_inputs(np.zeros((1, 1), np.float32), np.zeros((1, 1), BF), inputs, consts)
    for k in ("x3T", "x3bT"):
        p4.pop(k)
    m.update(p4)
    return m


def kernel_fused(**inputs):
    T = inputs["L0_attn_x"].shape[1]
    N = NCORES * T
    cap = max(int(math.ceil(math.ceil(N * 2 / 8) * 1.25)), 1)
    consts = host_constants()
    x = np.asarray(inputs["L0_attn_x"], np.float32)
    nc = build_fused(T, cap)
    maps = [fused_inputs(c, x, inputs, consts) for c in range(NCORES)]
    r = _run(nc, maps)
    return np.stack([np.ascontiguousarray(r[c]["x4T"].T) for c in range(NCORES)],
                    axis=0)


def kernel(**inputs):
    return kernel_fused(**inputs)
